# version 25
# baseline (speedup 1.0000x reference)
import numpy as np
import concourse.bass as bass
import concourse.mybir as mybir
from concourse.bass_utils import run_bass_kernel_spmd

F32 = mybir.dt.float32
BF16 = mybir.dt.bfloat16
Alu = mybir.AluOpType
Act = mybir.ActivationFunctionType
AX = mybir.AxisListType

CH = 128
NH = 16
HD = 64
DIN = 1024
CD = 1280
DM = 1024
DH = 512
JP = 2432
NJ = 19
DHID = 4096
EPS = 1e-5
MAGIC = 12582912.0
XBS = 516


def geom(kept=32, win=4):
    return dict(kept=kept, win=win, span=kept + 2 + 2 * win,
                f_hi=win + 1 + kept + 1, x1_chunks=kept + 2,
                out_rows=kept * CH)


def _tiles(lo, hi, w=4):
    t, c0 = [], lo
    while c0 < hi:
        t.append((c0, min(c0 + w, hi)))
        c0 += w
    return t


class Prog:
    ENG = ("T", "V", "S", "G", "Q")

    def __init__(self, nc):
        self.nc = nc
        self.ops = {e: [] for e in self.ENG}
        self.cnt = {}
        self.order = []
        self.last_wait = {e: {} for e in self.ENG}
        self.bank_readers = {b: [] for b in range(8)}

    def sem(self, name):
        if name not in self.cnt:
            self.cnt[name] = 0
            self.order.append(name)

    def ev(self, s):
        self.sem(s)
        return (s, self.cnt[s])

    NDMA = 6

    def op(self, eng, fn, after=(), sem=None, dma=False):
        if sem is None:
            sem = eng
        after = list(after)
        if dma:
            if not hasattr(self, "dma_rot"):
                self.dma_rot = {}
            idx = self.dma_rot.get(sem, 0)
            self.dma_rot[sem] = idx + 1
            sem = f"{sem}{idx % self.NDMA}"
            self.sem(sem)
            if self.cnt[sem] > 0:
                after.append((sem, self.cnt[sem]))
        self.sem(sem)
        inc = 16 if dma else 1
        waits = []
        la = self.last_wait[eng]
        for a in after:
            if a is None:
                continue
            (s, v) = a
            if v <= 0 or la.get(s, 0) >= v:
                continue
            la[s] = v
            waits.append((s, v))
        self.ops[eng].append((fn, waits, sem, inc))
        self.cnt[sem] += inc
        return (sem, self.cnt[sem])

    def mm(self, fn, banks, after=(), opens=False):
        extra = []
        if opens:
            for b in banks:
                extra += self.bank_readers[b]
                self.bank_readers[b] = []
        return self.op("T", fn, after=list(after) + extra)

    def rd(self, eng, fn, banks, after=(), **kw):
        e = self.op(eng, fn, after=after, **kw)
        for b in banks:
            self.bank_readers[b].append(e)
        return e

    def barrier(self):
        evs = [self.ev(s) for s in list(self.order)]
        for e in self.ENG:
            self.op(e, lambda eng: eng.nop(), after=list(evs))

    def realize(self):
        nc = self.nc
        handles = {name: nc.alloc_semaphore(name) for name in self.order}
        attr = {"T": "tensor", "V": "vector", "S": "scalar",
                "G": "gpsimd", "Q": "sync"}
        with nc.Block() as block:
            for ename in self.ENG:
                def body(eng, _e=ename):
                    for (fn, waits, sem, inc) in self.ops[_e]:
                        for (s, v) in waits:
                            eng.wait_ge(handles[s], v)
                        instr = fn(eng)
                        if instr is not None:
                            instr.then_inc(handles[sem], inc)
                getattr(block, attr[ename])(body)


def BK(PSUM, b, lo=0, hi=512):
    return PSUM[:, 512 * b + lo:512 * b + hi]


def build_nc(g):
    nc = bass.Bass("TRN2", target_bir_lowering=False)
    span_rows = g["span"] * CH
    x1_rows = g["x1_chunks"] * CH
    T = {}

    def din(name, shape, dtype=F32):
        T[name] = nc.dram_tensor(name, shape, dtype, kind="ExternalInput").ap()

    import os
    _dbg = os.environ.get("K_DBG") == "1"

    def dint(name, shape, dtype=F32):
        kind = "ExternalOutput" if _dbg else "Internal"
        T[name] = nc.dram_tensor(name, shape, dtype, kind=kind).ap()

    din("xc", [span_rows, DM])
    din("in_wT_f", [DH, JP]); din("in_wT_b", [DH, JP])
    din("norm_col", [DM, 1])
    for d in ("f", "b"):
        din(f"conv_w_{d}", [10, 128, 4]); din(f"conv_b_{d}", [10, 128, 1])
        din(f"dt_bias_{d}", [NH, 1]); din(f"A_log_{d}", [NH, 1])
        din(f"D64_{d}", [CH, 8])
        din(f"outp_{d}", [DH, DIN])
        din(f"gnorm_row_{d}", [1, DIN])
    din("op_wT", [DM, DM])
    din("opb_row", [1, DM])
    din("w1T", [DM, DHID]); din("w2T", [DHID, DM])
    din("fconv_w", [32, 128, 3]); din("fconv_b", [32, 128, 1])
    din("halo_mask", [x1_rows, 1])
    din("I128b", [128, 128], BF16)
    din("I128f", [128, 128], F32)
    din("MASKS", [128, 2 * 128], BF16)
    din("SELB", [NH, 8 * 128], BF16)
    din("ones_col", [128, 1])
    din("ones_row", [1, 128])

    T["y"] = nc.dram_tensor("y", [g["out_rows"], DM], F32,
                            kind="ExternalOutput").ap()

    dint("attn_f", [x1_rows, DM])
    dint("wf_b", [128, 8 * DM], BF16)
    dint("pack_xbc", [CD, x1_rows], BF16)
    dint("pack_dt", [NH, x1_rows])
    dint("pack_z", [DIN, x1_rows], BF16)
    dint("x1s", [x1_rows, DM])
    dint("h2qs", [DHID, x1_rows], BF16)
    dint("s2fs", [x1_rows, 1])

    p = Prog(nc)
    PSUM = nc.alloc_psum_tensor("PSUM", [128, 4096], F32).ap()
    PSUMb = PSUM.bitcast(BF16)

    st = {"off": 18432, "marks": []}
    sb = {}

    def asb(name, shape, dtype, at=None):
        nbytes = int(np.prod(shape[1:])) * mybir.dt.size(dtype)
        off = at if at is not None else (st["off"] + 63) // 64 * 64
        h = nc.alloc_sbuf_tensor_at(name, list(shape), dtype, offset=off)
        if at is None:
            st["off"] = off + nbytes
            assert st["off"] <= 229200, (name, st["off"])
        sb[name] = h.ap()
        return sb[name]
    c2_cur = lambda: st["off"]

    def mark():
        st["marks"].append(st["off"])

    def reset():
        st["off"] = st["marks"].pop()

    asb("I128b", [128, 128], BF16)
    asb("I128f", [128, 128], F32)
    asb("ones_col", [128, 1], F32)
    asb("ones_row", [1, 128], F32)

    c = dict(nc=nc, p=p, g=g, T=T, sb=sb, asb=asb, mark=mark, reset=reset,
             PSUM=PSUM, PSUMb=PSUMb, x1_rows=x1_rows, cur_off=c2_cur)

    mark()
    alloc_A(c)
    import os
    stop = os.environ.get("K_STOP", "")
    emit_weight_prep(c)
    emit_A1(c)
    p.barrier()
    if stop != "A1":
        emit_A2(c)
        p.barrier()
        reset()
        if stop != "A2":
            emit_B(c)

    p.realize()
    return nc


def alloc_A(c):
    asb, mark, reset = c["asb"], c["mark"], c["reset"]
    cur = c["cur_off"]
    c["ov_f"] = (cur() + 63) // 64 * 64
    asb("inwT_f", [128, 4 * JP], BF16)
    c["ov_b"] = (cur() + 63) // 64 * 64
    asb("inwT_b", [128, 4 * JP], BF16)
    asb("Wf", [128, 8 * DM], BF16)
    asb("opb_bcast", [128, DM], F32)
    asb("normc", [128, 8], F32)
    for d in ("f", "b"):
        asb(f"convw_{d}", [128, 10 * 4], F32)
        asb(f"convb_{d}", [128, 10], F32)
        asb(f"dtb_{d}", [NH, 1], F32)
        asb(f"negA_{d}", [NH, 1], F32)
        asb(f"D64_{d}", [128, 8], F32)
    asb("MASKS", [128, 2 * 128], BF16)
    asb("SELB", [NH, 8 * 128], BF16)
    c["wp_off"] = (c["cur_off"]() + 63) // 64 * 64
    asb("x0_tm", [128, 2 * DM], F32)
    asb("xnn_t", [128, DM], BF16)
    asb("xnn", [128, 8 * 512], BF16)
    asb("zf", [128, 8 * 512], BF16)
    asb("xbc_pre", [128, 10 * XBS], BF16)
    asb("dtraw", [NH, 512], F32)
    asb("dtraw_b", [NH, 512], F32)
    asb("dtsp", [NH, 512], F32)
    asb("dtA16", [NH, 512], F32)
    asb("cumA", [NH, 512], F32)
    asb("negcA", [NH, 512], F32)
    asb("revA", [NH, 512], F32)
    asb("dinb16", [NH, 512], BF16)
    asb("dexpb16", [NH, 512], BF16)
    asb("S32", [128, DIN], F32)
    asb("sm", [128, 4 * 32], F32)
    asb("dt_tm", [128, NH], F32)
    asb("dexp_tm", [128, NH], F32)
    asb("dec_sb", [128, NH], BF16)
    asb("relud", [128, 1024], F32)
    asb("x_tm", [128, DIN], BF16)
    asb("Gm", [128, 128], BF16)
    asb("dg", [128, 128], F32)
    asb("sumsq", [128, 1], F32)
    for sx in ("", "2"):
        asb("cumones" + sx, [2, 16 * 128], F32)
        asb("negcumones" + sx, [2, 16 * 128], F32)
        asb("decrow" + sx, [1, NH], F32)
        asb("Dexp" + sx, [128, 16 * 128], BF16)
        asb("B_tm" + sx, [128, 128], BF16)
        asb("dtx" + sx, [128, DIN], BF16)
        asb("dtxd" + sx, [128, DIN], BF16)
        asb("Sb" + sx, [128, DIN], BF16)
        asb("dinb" + sx, [128, DIN], BF16)
        asb("yg" + sx, [128, DIN], BF16)
        asb("rstd" + sx, [128, 1], F32)
        asb("attn_st" + sx, [128, DM], F32)
    asb("accV", [128, 512], BF16)
    asb("accG", [128, 512], BF16)
    asb("ttail", [128, 30], BF16)

    asb("xbc_pre3", [128, 10 * XBS], BF16, at=c["ov_f"])
    asb("zf3", [128, 8 * 512], BF16, at=c["ov_f"] + 10336)
    ob = c["ov_b"]
    for k, (n, dt) in enumerate([("dtraw3", F32), ("dtsp3", F32),
                                 ("dtA163", F32), ("cumA3", F32),
                                 ("negcA3", F32), ("revA3", F32)]):
        asb(n, [NH, 512], dt, at=ob + k * 2048)
    asb("dinb163", [NH, 512], BF16, at=ob + 12288)
    asb("dexpb163", [NH, 512], BF16, at=ob + 13312)
    ao = (cur() + 63) // 64 * 64
    asb("attnf_in", [128, DM], F32)
    asb("rp", [128, DM], F32)
    asb("pk_rot", [128, 4 * 512], BF16, at=ao)


def emit_weight_prep(c):
    nc, p, T, sb, PSUM = c["nc"], c["p"], c["T"], c["sb"], c["PSUM"]
    asb, mark, reset = c["asb"], c["mark"], c["reset"]

    def ld(dst, src, after=()):
        return p.op("Q", lambda q: q.dma_start(
            out=dst, in_=src, allow_slow_non_contiguous=True), after=after,
            sem="QD", dma=True)

    def ldg(dst, src, after=()):
        return p.op("G", lambda g_: g_.dma_start(
            out=dst, in_=src, allow_slow_non_contiguous=True),
            after=after, sem="GD", dma=True)

    ld(sb["I128b"][:], T["I128b"][:])
    ld(sb["I128f"][:], T["I128f"][:])
    ld(sb["ones_col"][:], T["ones_col"][:])
    ld(sb["ones_row"][:], T["ones_row"][:])
    ld(sb["opb_bcast"][:], T["opb_row"].partition_broadcast(128))
    ld(sb["MASKS"][:], T["MASKS"][:])
    ld(sb["SELB"][:], T["SELB"][:])
    e_n = p.op("Q", lambda q: q.dma_start(
        out=sb["normc"][:].rearrange("p (c o) -> p c o", c=8),
        in_=T["norm_col"].rearrange("(c p) o -> p c o", p=128),
        allow_slow_non_contiguous=True), sem="QD", dma=True)
    for d in ("f", "b"):
        ld(sb[f"convw_{d}"][:].rearrange("p (c k) -> p c k", c=10),
           T[f"conv_w_{d}"].rearrange("c p k -> p c k"))
        ld(sb[f"convb_{d}"][:].rearrange("p (c o) -> p c o", c=10),
           T[f"conv_b_{d}"].rearrange("c p o -> p c o"))
        ld(sb[f"dtb_{d}"][:], T[f"dt_bias_{d}"][:])
        ld(sb[f"D64_{d}"][:], T[f"D64_{d}"][:])
        e_a = ld(sb[f"negA_{d}"][:], T[f"A_log_{d}"][:])
        e1 = p.op("S", lambda s, d=d: s.activation(
            sb[f"negA_{d}"][:], sb[f"negA_{d}"][:], Act.Exp), after=[e_a])
        p.op("V", lambda v, d=d: v.tensor_scalar_mul(
            sb[f"negA_{d}"][:], sb[f"negA_{d}"][:], -1.0), after=[e1])

    for d, lo in (("f", 0), ("b", 4)):
        w = sb[f"inwT_{d}"]
        ec = ldg(w[:].rearrange("p (c j) -> p c j", c=4),
                 T[f"in_wT_{d}"].rearrange("(c p) j -> p c j", p=128))
        for cc in range(4):
            p.op("V", lambda v, w=w, cc=cc, lo=lo: v.tensor_scalar_mul(
                w[:, cc * JP:(cc + 1) * JP], w[:, cc * JP:(cc + 1) * JP],
                sb["normc"][:, lo + cc:lo + cc + 1]), after=[ec, e_n])

    wo = c["wp_off"]
    outp = asb("wp_outp", [128, 4 * DIN], F32, at=wo)
    outpb = asb("wp_outpb", [128, 4 * DIN], BF16, at=wo + 16384)
    gn = asb("wp_gn", [128, DIN], F32, at=wo + 16384 + 8192)
    opw = asb("wp_opw", [128, 8 * DM], BF16, at=wo + 16384 + 8192 + 4096)
    e3 = ldg(opw[:].rearrange("p (c j) -> p c j", c=8),
             T["op_wT"].rearrange("(c p) j -> p c j", p=128))
    prev_rd = []
    estore = None
    Wdst = sb["Wf"]
    for d, lo in (("b", 4), ("f", 0)):
        e1 = ld(outp[:].rearrange("p (c j) -> p c j", c=4),
                T[f"outp_{d}"].rearrange("(c p) j -> p c j", p=128),
                after=prev_rd)
        e2 = ld(gn[:], T[f"gnorm_row_{d}"].partition_broadcast(128),
                after=prev_rd)
        evs = []
        for cc in range(4):
            evs.append(p.op("V", lambda v, cc=cc: v.tensor_mul(
                outpb[:, cc * DIN:(cc + 1) * DIN],
                outp[:, cc * DIN:(cc + 1) * DIN], gn[:]), after=[e1, e2]))
        prev_rd = list(evs)
        wf_evs = []
        for hp in range(8):
            for half in range(2):
                bk = (hp * 2 + half) % 4
                last = None
                for cc in range(4):
                    def mmf(t, hp=hp, half=half, cc=cc, bk=bk, lo=lo):
                        return t.matmul(
                            BK(PSUM, bk),
                            outpb[:, cc * DIN + hp * 128:cc * DIN + hp * 128 + 128],
                            opw[:, (lo + cc) * DM + half * 512:
                                (lo + cc) * DM + half * 512 + 512],
                            start=(cc == 0), stop=(cc == 3))
                    last = p.mm(mmf, [bk], after=evs + [e3], opens=(cc == 0))
                ev = p.rd("V", lambda v, hp=hp, half=half, bk=bk:
                          v.tensor_copy(Wdst[:, hp * DM + half * 512:
                                             hp * DM + half * 512 + 512],
                                        BK(PSUM, bk)), [bk],
                          after=[last] + ([estore] if estore else []))
                wf_evs.append(ev)
                prev_rd.append(ev)
        if d == "b":
            estore = p.op("Q", lambda q: q.dma_start(
                out=T["wf_b"][:], in_=Wdst[:]), after=wf_evs,
                sem="QD", dma=True)
    p.barrier()


def emit_stats(c, src, smb, after, want_fused=True, want_rp=False):
    p, sb = c["p"], c["sb"]
    sm = sb["sm"]

    def S(i):
        return sm[:, smb + i:smb + i + 1]
    sm_ = sb["sm"]
    b1 = p.op("V", lambda v: v.bn_stats(sm_[:, smb + 16:smb + 22],
                                        src[:, 0:512]), after=after)
    b2 = p.op("V", lambda v: v.bn_stats(sm_[:, smb + 22:smb + 28],
                                        src[:, 512:1024]), after=after)
    e1 = p.op("V", lambda v: v.bn_aggr(sm_[:, smb + 4:smb + 6],
                                       sm_[:, smb + 16:smb + 28]),
              after=[b1, b2])
    e_m = b1
    e_s = b2
    e5 = p.op("V", lambda v: v.tensor_scalar_add(S(6), S(5), EPS), after=[e1])
    e6 = p.op("S", lambda s: s.activation(S(7), S(6), Act.Ln), after=[e5])
    e7 = p.op("S", lambda s: s.activation(S(8), S(7), Act.Exp, scale=-0.5),
              after=[e6])
    out = dict(mean=e1, ln=e7, src_done=[e_m, e_s])
    if want_fused:
        e8 = p.op("V", lambda v: v.tensor_mul(S(9), S(8), S(8)), after=[e7])
        e8b = p.op("V", lambda v: v.tensor_mul(S(10), S(5), S(9)), after=[e8])
        e9 = p.op("V", lambda v: v.tensor_scalar_add(S(11), S(10), EPS),
                  after=[e8b])
        e10 = p.op("S", lambda s: s.activation(S(12), S(11), Act.Ln),
                   after=[e9])
        e11 = p.op("S", lambda s: s.activation(S(12), S(12), Act.Exp,
                                               scale=-0.5), after=[e10])
        out["fused"] = p.op("V", lambda v: v.tensor_mul(S(13), S(8), S(12)),
                            after=[e7, e11])
    if want_rp:
        ea = p.op("V", lambda v: v.tensor_scalar_add(S(14), S(8), 1.0),
                  after=[e7])
        eb = p.op("V", lambda v: v.tensor_mul(S(15), S(4), S(8)),
                  after=[e7, e1])
        out["rp"] = [ea, eb]
    return out


def emit_tile_front(c, c0, w, dirs, xbc_doff=4, prev_chunk_consumers=()):
    nc, p, T, sb, PSUM, PSUMb = (c["nc"], c["p"], c["T"], c["sb"], c["PSUM"],
                                 c["PSUMb"])
    res = {}
    ev_xnn = []
    prev_tp = list(prev_chunk_consumers)
    pair_prev = list(prev_chunk_consumers)
    pair_cur = []
    eq = None
    for cc in range(w):
        if cc % 2 == 0:
            w2 = min(2, w - cc)
            eq = p.op("Q", lambda q, cc=cc, w2=w2: q.dma_start(
                out=sb["x0_tm"][:, 0:w2 * DM].rearrange("p (c d) -> p c d", c=w2),
                in_=T["xc"][(c0 + cc) * CH:(c0 + cc + w2) * CH, :]
                .rearrange("(c p) d -> p c d", p=128)),
                after=pair_prev + pair_cur, sem="QD", dma=True)
            pair_prev = pair_cur
            pair_cur = []
        x0c = sb["x0_tm"][:, (cc % 2) * DM:(cc % 2) * DM + DM]
        stt = emit_stats(c, x0c, (cc % 4) * 32, [eq], want_fused=True,
                         want_rp=True)
        sm = sb["sm"]
        smb = (cc % 4) * 32
        e13 = p.op("V", lambda v, x0c=x0c, smb=smb: v.tensor_scalar(
            sb["xnn_t"][:], x0c, sm[:, smb + 4:smb + 5], sm[:, smb + 13:smb + 14],
            Alu.subtract, Alu.mult), after=[stt["fused"], stt["mean"]] + prev_tp)
        pair_cur += [e13] + stt["src_done"]
        res.setdefault("rp", {})[cc] = stt.get("rp", [])
        prev_tp = []
        for b8 in range(8):
            bk = 6 + b8 % 2
            et = p.mm(lambda t, b8=b8, bk=bk: t.transpose(
                PSUMb[:, bk * 1024:bk * 1024 + 128],
                sb["xnn_t"][:, b8 * 128:b8 * 128 + 128], sb["I128b"][:]),
                [bk], after=[e13], opens=True)
            prev_tp.append(et)
            ev_xnn.append(p.rd("V", lambda v, b8=b8, bk=bk, cc=cc: v.tensor_copy(
                sb["xnn"][:, b8 * 512 + cc * 128:b8 * 512 + cc * 128 + 128],
                PSUMb[:, bk * 1024:bk * 1024 + 128]), [bk], after=[et]))
    res["ln"] = list(ev_xnn)
    res["x0"] = eq

    for (d, mode, zdst, xdst, xoff) in dirs:
        wmat = sb[f"inwT_{d}"]
        blo = 0 if d == "f" else 4
        jlist = ([NJ - 1] + list(range(NJ - 1)) if mode == "full"
                 else [NJ - 1] + list(range(8, NJ - 1)))
        ev_ev = []
        ev_dt = None
        for j in jlist:
            bk = j % 4
            last = None
            for ccn in range(4):
                def mmf(t, j=j, ccn=ccn, bk=bk, w=w, blo=blo, wmat=wmat):
                    return t.matmul(
                        BK(PSUM, bk, 0, w * 128),
                        wmat[:, ccn * JP + j * 128:ccn * JP + j * 128 + 128],
                        sb["xnn"][:, (blo + ccn) * 512:(blo + ccn) * 512 + w * 128],
                        start=(ccn == 0), stop=(ccn == 3))
                last = p.mm(mmf, [bk], after=ev_xnn, opens=(ccn == 0))
            pack_dir = zdst is not None and zdst is not sb["zf"]
            if pack_dir and j < 18:
                pst = c["pk_state"]
                slot = pst["idx"] % 4
                pst["idx"] += 1
                war = [e for e in [pst["dma"][slot]] if e is not None]
                if j < 8:
                    rows = ("pack_z", j * 128)
                    fn = Act.Silu
                else:
                    rows = ("pack_xbc", (j - 8) * 128)
                    fn = Act.Copy
                ev = p.rd("S", lambda s, bk=bk, w=w, slot=slot, fn=fn:
                          s.activation(
                              sb["pk_rot"][:, slot * 512:slot * 512 + w * 128],
                              BK(PSUM, bk, 0, w * 128), fn),
                          [bk], after=[last] + war)
                ed = p.op("Q", lambda q, rows=rows, slot=slot, w=w,
                          x1o=pst["x1o"]:
                          q.dma_start(
                              out=T[rows[0]][rows[1]:rows[1] + 128,
                                             x1o:x1o + w * 128],
                              in_=sb["pk_rot"][:, slot * 512:slot * 512 + w * 128]),
                          after=[ev], sem="QD", dma=True)
                pst["dma"][slot] = ed
                ev_ev.append(ed)
                continue
            if j < 8:
                if pack_dir:
                    ev_ev.append(p.rd("S", lambda s, j=j, bk=bk, w=w, zdst=zdst:
                                      s.activation(
                                          zdst[:, j * 512:j * 512 + w * 128],
                                          BK(PSUM, bk, 0, w * 128), Act.Copy),
                                      [bk], after=[last]))
                else:
                    ev_ev.append(p.rd("S", lambda s, j=j, bk=bk, w=w,
                                      zdst=zdst: s.activation(
                                          zdst[:, j * 512:j * 512 + w * 128],
                                          BK(PSUM, bk, 0, w * 128), Act.Copy),
                                      [bk], after=[last]))
            elif j < 18:
                stride = XBS if xdst is sb["xbc_pre"] else 512
                if pack_dir:
                    ev_ev.append(p.rd("S", lambda s, j=j, bk=bk, w=w, xdst=xdst,
                                      xoff=xoff, stride=stride: s.activation(
                                          xdst[:, (j - 8) * stride + xoff:
                                               (j - 8) * stride + xoff + w * 128],
                                          BK(PSUM, bk, 0, w * 128), Act.Copy),
                                      [bk], after=[last]))
                else:
                    ev_ev.append(p.rd("S", lambda s, j=j, bk=bk, w=w,
                                      xdst=xdst, xoff=xoff, stride=stride:
                                      s.activation(
                                          xdst[:, (j - 8) * stride + xoff:
                                               (j - 8) * stride + xoff + w * 128],
                                          BK(PSUM, bk, 0, w * 128), Act.Copy),
                                      [bk], after=[last]))
            else:
                dtdst = sb["dtraw_b"] if (d == "b" and mode == "full") else sb["dtraw"]
                ev_dt = p.rd("V", lambda v, bk=bk, w=w, dtdst=dtdst: v.tensor_copy(
                    dtdst[:, 0:w * 128],
                    PSUM[0:16, bk * 512:bk * 512 + w * 128]), [bk], after=[last])
                ev_ev.append(ev_dt)
        res[d] = dict(evac=ev_ev, dt=ev_dt)
    return res


def emit_conv_silu(c, d, w, src_evs):
    p, sb = c["p"], c["sb"]
    ts = c.get("tsfx", "")
    nts = c.get("tsfx_next", ts)

    def TB(n, s=None):
        return sb[n + (ts if s is None else s)]
    fwd = (d == "f")
    cw, cb = sb[f"convw_{d}"], sb[f"convb_{d}"]
    doff = 4 if fwd else (512 - w * 128)
    offs = [1, 2, 3, 4] if fwd else [doff, doff + 1, doff + 2, doff + 3]
    wof = [0, 1, 2, 3] if fwd else [3, 2, 1, 0]
    src_lo = (4 + w * 128 - 3) if fwd else doff
    e_st = p.op("V", lambda v: v.tensor_copy(
        sb["ttail"][:].rearrange("p (b s) -> p b s", b=10),
        TB("xbc_pre").rearrange("p (b s) -> p b s", b=10)
        [:, :, src_lo:src_lo + 3]), after=list(src_evs))
    evs = []
    tap_evs = []
    ACCN = ["accV", "accG"]
    last_sil = {i: None for i in range(2)}
    for blk in range(10):
        acc = sb[ACCN[blk % 2]]
        pe = last_sil[blk % 2]
        e = p.op("V", lambda v, blk=blk, w=w, acc=acc: v.tensor_scalar(
            acc[:, 0:w * 128],
            TB("xbc_pre")[:, blk * XBS + offs[0]:blk * XBS + offs[0] + w * 128],
            cw[:, blk * 4 + wof[0]:blk * 4 + wof[0] + 1],
            cb[:, blk:blk + 1], Alu.mult, Alu.add),
            after=list(src_evs) + [e_st] + ([pe] if pe else []))
        for k in range(1, 4):
            e = p.op("V", lambda v, blk=blk, k=k, w=w, acc=acc:
                     v.scalar_tensor_tensor(
                acc[:, 0:w * 128],
                TB("xbc_pre")[:, blk * XBS + offs[k]:blk * XBS + offs[k] + w * 128],
                cw[:, blk * 4 + wof[k]:blk * 4 + wof[k] + 1],
                acc[:, 0:w * 128], Alu.mult, Alu.add), after=[e])
        tap_evs.append(e)
        se = p.op("S", lambda s, blk=blk, w=w, acc=acc: s.activation(
            TB("xbc_pre")[:, blk * XBS + doff:blk * XBS + doff + w * 128],
            acc[:, 0:w * 128], Act.Silu), after=[e])
        last_sil[blk % 2] = se
        evs.append(se)
    slot_lo = 1 if fwd else 512
    tail_ev = p.op("V", lambda v: v.tensor_copy(
        TB("xbc_pre", nts).rearrange("p (b s) -> p b s", b=10)
        [:, :, slot_lo:slot_lo + 3],
        sb["ttail"][:].rearrange("p (b s) -> p b s", b=10)),
        after=tap_evs + [e_st])
    return evs, tail_ev, doff


def emit_dt_pipeline(c, d, w, ev_dt):
    p, sb = c["p"], c["sb"]
    ts = c.get("tsfx", "")

    def TB(n):
        return sb[n + ts]
    e1 = p.op("S", lambda s: s.activation(TB("dtsp")[:, 0:w * 128],
                                          TB("dtraw")[:, 0:w * 128], Act.Exp,
                                          bias=sb[f"dtb_{d}"][:]), after=[ev_dt])
    e2 = p.op("S", lambda s: s.activation(TB("dtsp")[:, 0:w * 128],
                                          TB("dtsp")[:, 0:w * 128], Act.Ln,
                                          bias=1.0), after=[e1])
    e3 = p.op("V", lambda v: v.tensor_scalar_mul(TB("dtA16")[:, 0:w * 128],
                                                 TB("dtsp")[:, 0:w * 128],
                                                 sb[f"negA_{d}"][:]), after=[e2])
    e4 = e3
    for cl in range(w):
        e4 = p.op("V", lambda v, cl=cl: v.tensor_tensor_scan(
            TB("cumA")[:, cl * 128:(cl + 1) * 128],
            TB("dtA16")[:, cl * 128:(cl + 1) * 128],
            TB("dtA16")[:, cl * 128:(cl + 1) * 128], 0.0,
            Alu.add, Alu.bypass), after=[e4])
    e6 = p.op("V", lambda v: v.scalar_tensor_tensor(
        TB("revA")[:, 0:w * 128].rearrange("p (c t) -> p c t", c=w),
        TB("cumA")[:, 0:w * 128].rearrange("p (c t) -> p c t", c=w), -1.0,
        TB("cumA")[:, 0:w * 128].rearrange("p (c t) -> p c t", c=w)
        [:, :, 127:128].broadcast_to([NH, w, 128]),
        Alu.mult, Alu.add), after=[e4])
    if d == "b":
        c1 = p.op("V", lambda v: v.tensor_sub(TB("negcA")[:, 0:w * 128],
                                              TB("cumA")[:, 0:w * 128],
                                              TB("dtA16")[:, 0:w * 128]),
                  after=[e4])
        c2 = p.op("V", lambda v: v.tensor_add(TB("cumA")[:, 0:w * 128],
                                              TB("revA")[:, 0:w * 128],
                                              TB("dtA16")[:, 0:w * 128]),
                  after=[e6, c1])
        c3 = p.op("V", lambda v: v.tensor_copy(TB("revA")[:, 0:w * 128],
                                               TB("negcA")[:, 0:w * 128]),
                  after=[c1, c2])
        e4 = c2
        e6 = c3
    e5 = p.op("V", lambda v: v.tensor_scalar_mul(TB("negcA")[:, 0:w * 128],
                                                 TB("cumA")[:, 0:w * 128],
                                                 -1.0), after=[e4, e6])
    e7 = p.op("S", lambda s: s.activation(TB("dinb16")[:, 0:w * 128],
                                          TB("cumA")[:, 0:w * 128], Act.Exp),
              after=[e4])
    e8 = p.op("S", lambda s: s.activation(TB("dexpb16")[:, 0:w * 128],
                                          TB("revA")[:, 0:w * 128], Act.Exp),
              after=[e6])
    return e2, dict(dtA=e3, cum=e4, neg=e5, rev=e6, din=e7, dexp=e8)


def emit_zsilu(c, w, src_ev):
    p, sb = c["p"], c["sb"]
    ts = c.get("tsfx", "")

    def TB(n):
        return sb[n + ts]
    evs = []
    for b8 in range(8):
        evs.append(p.op("S", lambda s, b8=b8, w=w: s.activation(
            TB("zf")[:, b8 * 512:b8 * 512 + w * 128],
            TB("zf")[:, b8 * 512:b8 * 512 + w * 128], Act.Silu),
            after=list(src_ev)))
    return evs


def ssd_front(c, d, o, full, deps, par, xbase=4):
    nc, p, sb, PSUM, PSUMb = c["nc"], c["p"], c["sb"], c["PSUM"], c["PSUMb"]
    fwd = (d == "f")
    MSK = sb["MASKS"][:, 0:128] if fwd else sb["MASKS"][:, 128:256]
    I128f, I128b = sb["I128f"], sb["I128b"]
    conv_evs = deps["conv"]
    B = 4 * par
    sfx = "2" if par else ""
    ts = c.get("tsfx", "")

    def P(n):
        return sb[n + sfx]

    def TB(n):
        return sb[n + ts]

    e1 = p.mm(lambda t: t.transpose(BK(PSUM, B + 0, 0, 16),
                                    TB("dtsp")[:, o:o + 128],
                                    I128f[0:16, 0:16]),
              [B + 0], after=[deps["dt"]], opens=True)
    ev_dtm = p.rd("V", lambda v: v.tensor_copy(sb["dt_tm"][:],
                                               BK(PSUM, B + 0, 0, 16)),
                  [B + 0], after=[e1])
    e2 = p.mm(lambda t: t.transpose(
        PSUMb[:, (B + 1) * 1024:(B + 1) * 1024 + 16],
        TB("dexpb16")[:, o:o + 128], I128b[0:16, 0:16]),
        [B + 1], after=[deps["dexp"]], opens=True)
    ev_dexptm = p.rd("V", lambda v: v.tensor_copy(
        sb["dexp_tm"][:], PSUMb[:, (B + 1) * 1024:(B + 1) * 1024 + 16]),
        [B + 1], after=[e2])
    dcol = o + 127 if fwd else o
    e3 = p.mm(lambda t: t.transpose(
        PSUM[0:1, (B + 1) * 512 + 32:(B + 1) * 512 + 48],
        TB("cumA")[:, dcol:dcol + 1], I128f[0:16, 0:16]), [B + 1],
        after=[deps["cum"]], opens=True)
    e3b = p.rd("S", lambda s: s.activation(
        P("decrow")[:], PSUM[0:1, (B + 1) * 512 + 32:(B + 1) * 512 + 48],
        Act.Exp), [B + 1], after=[e3])
    e4 = p.mm(lambda t: t.matmul(BK(PSUM, B + 0, 16, 32), sb["ones_row"][:],
                                 P("decrow")[:], start=True, stop=True,
                                 skip_group_check=True),
              [B + 0], after=[e3b], opens=True)
    ev_dec = p.rd("V", lambda v: v.tensor_copy(sb["dec_sb"][:],
                                               BK(PSUM, B + 0, 16, 32)),
                  [B + 0], after=[e4])
    ev_xt = []
    for b8 in range(8):
        bk = B + 2 + (b8 % 2)
        et = p.mm(lambda t, b8=b8, bk=bk: t.transpose(
            PSUMb[:, bk * 1024:bk * 1024 + 128],
            TB("xbc_pre")[:, b8 * XBS + xbase + o:b8 * XBS + xbase + o + 128],
            I128b[:]), [bk], after=[conv_evs[b8]], opens=True)
        ev_xt.append(p.rd("V", lambda v, b8=b8, bk=bk: v.tensor_copy(
            sb["x_tm"][:, b8 * 128:b8 * 128 + 128],
            PSUMb[:, bk * 1024:bk * 1024 + 128]), [bk], after=[et]))
    et = p.mm(lambda t: t.transpose(
        PSUMb[:, (B + 2) * 1024:(B + 2) * 1024 + 128],
        TB("xbc_pre")[:, 8 * XBS + xbase + o:8 * XBS + xbase + o + 128],
        I128b[:]), [B + 2], after=[conv_evs[8]], opens=True)
    ev_bt = p.rd("V", lambda v: v.tensor_copy(
        P("B_tm")[:], PSUMb[:, (B + 2) * 1024:(B + 2) * 1024 + 128]),
        [B + 2], after=[et])
    ev_dtx = p.op("V", lambda v: v.tensor_tensor(
        P("dtx").rearrange("p (h q) -> p h q", h=NH),
        sb["x_tm"].rearrange("p (h q) -> p h q", h=NH),
        sb["dt_tm"][:].unsqueeze(2).broadcast_to([128, NH, HD]), Alu.mult),
        after=ev_xt + [ev_dtm])
    ev_dtxd = p.op("V", lambda v: v.tensor_tensor(
        P("dtxd").rearrange("p (h q) -> p h q", h=NH),
        P("dtx").rearrange("p (h q) -> p h q", h=NH),
        sb["dexp_tm"][:].unsqueeze(2).broadcast_to([128, NH, HD]),
        Alu.mult), after=[ev_dtx, ev_dexptm])
    out = dict(ev_dtxd=ev_dtxd, ev_bt=ev_bt, ev_dec=ev_dec)
    if full:
        stw = [deps["stk_war"]] if deps.get("stk_war") else []
        esA = p.op("G", lambda g_: g_.dma_start(
            out=P("cumones")[1:2, :].rearrange("o (h s) -> o h s", h=16),
            in_=TB("cumA")[:, o:o + 128]), after=[deps["cum"]] + stw,
            sem="GD", dma=True)
        esB = p.op("G", lambda g_: g_.dma_start(
            out=P("negcumones")[0:1, :].rearrange("o (h s) -> o h s", h=16),
            in_=TB("negcA")[:, o:o + 128]), after=[deps["neg"]] + stw,
            sem="GD", dma=True)
        ev_ex = []
        for half in range(2):
            mm_d = []
            for bq in range(2):
                bk = B + bq
                for hq in range(4):
                    h = half * 8 + bq * 4 + hq
                    def mma(t, h=h, bk=bk, hq=hq):
                        return t.matmul(
                            BK(PSUM, bk, hq * 128, hq * 128 + 128),
                            P("cumones")[0:2, h * 128:(h + 1) * 128],
                            P("negcumones")[0:2, h * 128:(h + 1) * 128],
                            start=(hq == 0), stop=(hq == 3))
                    mm_d.append(p.mm(mma, [bk], after=[esA, esB],
                                     opens=(hq == 0)))
            rl = [c["relud_last"]] if c.get("relud_last") else []
            er = p.rd("S", lambda s: s.activation(
                sb["relud"][:], PSUM[:, B * 512:B * 512 + 1024],
                Act.Relu), [B + 0, B + 1], after=mm_d + rl)
            ev_ex.append(p.op("S", lambda s, half=half: s.activation(
                P("Dexp")[:, half * 1024:half * 1024 + 1024], sb["relud"][:],
                Act.Exp, scale=-1.0), after=[er]))
            c["relud_last"] = ev_ex[-1]
        out["stk"] = ev_ex[-1]
        e7 = p.mm(lambda t: t.matmul(
            BK(PSUM, B + 2, 0, 128),
            TB("xbc_pre")[:, 8 * XBS + xbase + o:8 * XBS + xbase + o + 128],
            TB("xbc_pre")[:, 9 * XBS + xbase + o:9 * XBS + xbase + o + 128],
            start=True, stop=True), [B + 2],
            after=[conv_evs[8], conv_evs[9]], opens=True)
        ev_gm = p.rd("V", lambda v: v.tensor_tensor(
            sb["Gm"][:], BK(PSUM, B + 2, 0, 128), MSK, Alu.mult),
            [B + 2], after=[e7])
        out["ev_m"] = p.op("V", lambda v: v.tensor_tensor(
            P("Dexp").rearrange("p (h q) -> p h q", h=NH),
            P("Dexp").rearrange("p (h q) -> p h q", h=NH),
            sb["Gm"][:].unsqueeze(1).broadcast_to([128, NH, 128]), Alu.mult),
            after=ev_ex + [ev_gm])
    mm_s = []
    for half in range(2):
        bk = B + 2 + half
        def mmf(t, half=half, bk=bk):
            return t.matmul(BK(PSUM, bk, 0, 512), P("B_tm")[:],
                            P("dtxd")[:, half * 512:half * 512 + 512],
                            start=True, stop=True)
        mm_s.append(p.mm(mmf, [bk], after=[ev_dtxd, ev_bt], opens=True))
    out["mm_s"] = mm_s
    return out


def ssd_back(c, d, o, full, fr, deps, attn_fn, par, xbase=4):
    nc, p, sb, PSUM, PSUMb = c["nc"], c["p"], c["sb"], c["PSUM"], c["PSUMb"]
    B = 4 * par
    sfx = "2" if par else ""
    osfx = "" if par else "2"
    ts = c.get("tsfx", "")

    def P(n):
        return sb[n + sfx]

    def TB(n):
        return sb[n + ts]
    Sb_prev = sb["Sb" + osfx]
    conv_evs = deps["conv"]
    mm_s = fr["mm_s"]
    emit_attn = attn_fn() if (full and attn_fn is not None) else None
    sbw = [deps["sb_war"]] if deps.get("sb_war") else []
    e1 = p.op("V", lambda v: v.tensor_tensor(
        P("Sb").rearrange("p (h q) -> p h q", h=NH),
        sb["S32"].rearrange("p (h q) -> p h q", h=NH),
        sb["dec_sb"][:].unsqueeze(2).broadcast_to([128, NH, HD]), Alu.mult),
        after=[fr["ev_dec"], deps["sb_ready"]] + sbw)
    e2 = p.rd("V", lambda v: v.tensor_add(sb["S32"][:, 0:512],
                                          P("Sb")[:, 0:512],
                                          BK(PSUM, B + 2, 0, 512)), [B + 2],
              after=[e1] + mm_s)
    e3 = p.rd("V", lambda v: v.tensor_add(sb["S32"][:, 512:1024],
                                          P("Sb")[:, 512:1024],
                                          BK(PSUM, B + 3, 0, 512)), [B + 3],
              after=[e1] + mm_s)
    e4 = p.op("V", lambda v: v.tensor_copy(P("Sb")[:], sb["S32"][:]),
              after=[e2, e3])
    res = dict(ev_state=e4, stk=fr.get("stk"), mm_z=[], attn=[])
    if not full:
        return res
    mm_din = []
    for ci in range(8):
        bk = B + ci // 4
        pos = (ci % 4) * 128
        def mmf(t, ci=ci, bk=bk, pos=pos):
            return t.matmul(BK(PSUM, bk, pos, pos + 128),
                            sb["SELB"][:, ci * 128:ci * 128 + 128],
                            TB("dinb16")[:, o:o + 128], start=(ci % 4 == 0),
                            stop=(ci % 4 == 3))
        mm_din.append(p.mm(mmf, [bk], after=[deps["din"]],
                           opens=(ci % 4 == 0)))
    ev_di1 = p.rd("V", lambda v: v.tensor_copy(P("dinb")[:, 0:512],
                                               BK(PSUM, B + 0, 0, 512)),
                  [B + 0], after=mm_din)
    ev_di2 = p.rd("V", lambda v: v.tensor_copy(P("dinb")[:, 512:1024],
                                               BK(PSUM, B + 1, 0, 512)),
                  [B + 1], after=mm_din)
    mm_z = []
    for ci in range(8):
        bk = B + 2 + ci // 4
        pos = (ci % 4) * 128
        def mmf(t, ci=ci, bk=bk, pos=pos):
            return t.matmul(
                BK(PSUM, bk, pos, pos + 128),
                Sb_prev[:, ci * 128:ci * 128 + 128],
                TB("xbc_pre")[:, 9 * XBS + xbase + o:9 * XBS + xbase + o + 128],
                start=(ci % 4 == 0), stop=(ci % 4 == 3))
        mm_z.append(p.mm(mmf, [bk],
                         after=[conv_evs[9], deps["sb_ready"]],
                         opens=(ci % 4 == 0)))
    res["mm_z"] = mm_z
    mm_y = []
    for h in range(NH):
        ci = h // 2
        bk = B + ci // 4
        pos = (ci % 4) * 128
        pb = 64 * (h % 2)
        def mmf(t, h=h, bk=bk, pos=pos, pb=pb):
            return t.matmul(
                PSUM[pb:pb + 64, bk * 512 + pos:bk * 512 + pos + 128],
                P("dtx")[:, h * 64:h * 64 + 64],
                P("Dexp")[:, h * 128:h * 128 + 128],
                start=True, stop=True, skip_group_check=True,
                tile_position=(0, pb))
        mm_y.append(p.mm(mmf, [bk], after=[fr["ev_m"], fr["ev_dtxd"]],
                         opens=(h in (0, 8))))
    d64 = sb[f"D64_{d}"]
    ev_g1a = p.rd("V", lambda v: v.tensor_tensor(
        P("dinb")[:, 0:512], P("dinb")[:, 0:512], BK(PSUM, B + 2, 0, 512),
        Alu.mult), [B + 2], after=mm_z + [ev_di1])
    ev_g1b = p.rd("V", lambda v: v.tensor_tensor(
        P("dinb")[:, 512:1024], P("dinb")[:, 512:1024],
        BK(PSUM, B + 3, 0, 512), Alu.mult), [B + 3], after=mm_z + [ev_di2])
    ev_g2a = p.rd("V", lambda v: v.tensor_add(
        P("dinb")[:, 0:512], P("dinb")[:, 0:512], BK(PSUM, B + 0, 0, 512)),
        [B + 0], after=mm_y + [ev_g1a])
    ev_g2b = p.rd("V", lambda v: v.tensor_add(
        P("dinb")[:, 512:1024], P("dinb")[:, 512:1024],
        BK(PSUM, B + 1, 0, 512)), [B + 1], after=mm_y + [ev_g1b])
    ev_dterm = p.op("V", lambda v: v.tensor_tensor(
        P("dtx").rearrange("p (a q) -> p a q", a=8),
        TB("xbc_pre").rearrange("p (a q) -> p a q", a=10)
        [:, 0:8, xbase + o:xbase + o + 128],
        d64[:].unsqueeze(2).broadcast_to([128, 8, 128]), Alu.mult),
        after=[ev_g2a, ev_g2b])
    ev_g3 = p.op("V", lambda v: v.tensor_add(P("dinb")[:], P("dinb")[:],
                                             P("dtx")[:]),
                 after=[ev_g2a, ev_g2b, ev_dterm])
    ev_yg = p.op("V", lambda v: v.tensor_tensor(
        P("yg").rearrange("p (a q) -> p a q", a=8),
        P("dinb").rearrange("p (a q) -> p a q", a=8),
        TB("zf").rearrange("p (a q) -> p a q", a=8)[:, :, o:o + 128],
        Alu.mult), after=[ev_g3, deps["z"]])
    mm_g = []
    for ci in range(8):
        def mmf(t, ci=ci):
            return t.matmul(BK(PSUM, B + 2, 0, 128),
                            P("yg")[:, ci * 128:ci * 128 + 128],
                            P("yg")[:, ci * 128:ci * 128 + 128],
                            start=(ci == 0), stop=(ci == 7))
        mm_g.append(p.mm(mmf, [B + 2], after=[ev_yg], opens=(ci == 0)))
    ev_dg = p.rd("V", lambda v: v.tensor_tensor(
        sb["dg"][:], BK(PSUM, B + 2, 0, 128), sb["I128f"][:], Alu.mult),
        [B + 2], after=mm_g)
    ev_ss = p.op("V", lambda v: v.reduce_sum(sb["sumsq"][:], sb["dg"][:],
                                             axis=AX.X), after=[ev_dg])
    ev_r1 = p.op("V", lambda v: v.tensor_scalar(P("rstd")[:], sb["sumsq"][:],
                                                1.0 / DIN, EPS, Alu.mult,
                                                Alu.add), after=[ev_ss])
    ev_r2 = p.op("S", lambda s: s.activation(P("rstd")[:], P("rstd")[:],
                                             Act.Ln), after=[ev_r1])
    ev_r3 = p.op("S", lambda s: s.activation(P("rstd")[:], P("rstd")[:],
                                             Act.Exp, scale=-0.5),
                 after=[ev_r2])
    Wf = sb["Wf"]
    wf_dep = c.get("wf_ev") or []
    mm_o = []
    for half in range(2):
        bk = B + half
        for ci in range(8):
            def mmf(t, half=half, ci=ci, bk=bk):
                return t.matmul(
                    BK(PSUM, bk, 0, 512),
                    P("yg")[:, ci * 128:ci * 128 + 128],
                    Wf[:, ci * DM + half * 512:ci * DM + half * 512 + 512],
                    start=(ci == 0), stop=(ci == 7))
            mm_o.append(p.mm(mmf, [bk], after=[ev_yg] + wf_dep,
                             opens=(ci == 0)))
    res["attn"] = emit_attn(mm_o, ev_r3, B, P("rstd"))
    return res


def run_pairs(c, st, d, order, evd, ev_sp, ev_conv, ev_z, full_fn, attn_mk,
              xbase=4):
    p = c["p"]
    tails = []
    for i in range(0, len(order), 2):
        grp = order[i:i + 2]
        frs = []
        for cl in grp:
            o = cl * 128
            par = st["seq"] % 2
            full = full_fn(cl)
            deps = dict(dt=ev_sp, cum=evd["cum"], neg=evd["neg"],
                        din=evd["din"], dexp=evd["dexp"], conv=ev_conv,
                        stk_war=st["stk"][par],
                        z=(ev_z[-1] if ev_z else None))
            fr = ssd_front(c, d, o, full, deps, par, xbase)
            frs.append((cl, o, par, full, fr, deps))
            st["seq"] += 1
        for (cl, o, par, full, fr, deps) in frs:
            deps["sb_ready"] = st["sb_ready"]
            deps["sb_war"] = st["sb_war"]
            bk = ssd_back(c, d, o, full, fr, deps,
                          (attn_mk(cl) if attn_mk else None), par, xbase)
            st["sb_ready"] = bk["ev_state"]
            if bk["mm_z"]:
                st["sb_war"] = bk["mm_z"][-1]
            if bk.get("stk"):
                st["stk"][par] = bk["stk"]
            tails.append(bk)
    return tails


def emit_A1(c):
    nc, p, g, T, sb, PSUM = c["nc"], c["p"], c["g"], c["T"], c["sb"], c["PSUM"]
    win, f_hi = g["win"], g["f_hi"]
    c["tsfx"] = ""
    c["tsfx_next"] = ""
    c["pk_state"] = dict(idx=0, dma=[None] * 4, x1o=0)
    for n in ("cumones", "negcumones", "cumones2", "negcumones2"):
        p.op("V", lambda v, n=n: v.memset(sb[n][0:2, :], 1.0))
    p.op("V", lambda v: v.memset(sb["S32"][:], 0.0))
    e0d = p.op("G", lambda g_: g_.memset(sb["Sb"][:], 0.0))
    e0d2 = p.op("G", lambda g_: g_.memset(sb["Sb2"][:], 0.0))
    e0e = p.op("V", lambda v: v.memset(sb["xbc_pre"][:], 0.0))
    st = dict(seq=0, sb_ready=e0d, sb_war=e0d2, stk={0: None, 1: None})
    aprev = {0: None, 1: None}
    prev_pack = []
    prev_tail = e0e
    prev_consumers = []
    for (c0, c1) in _tiles(0, f_hi):
        w = c1 - c0
        full_tile = c1 > win
        dirs = [("f", "full" if full_tile else "xbcdt", sb["zf"],
                 sb["xbc_pre"], 4)]
        packing = c0 >= win
        if packing:
            c["pk_state"]["x1o"] = (c0 - win) * CH
            dirs.append(("b", "full", sb["pk_rot"], sb["pk_rot"], 0))
        fr = emit_tile_front(c, c0, w, dirs,
                             prev_chunk_consumers=prev_consumers)
        ev_conv, prev_tail_new, _doff = emit_conv_silu(
            c, "f", w, fr["f"]["evac"] + [prev_tail])
        ev_sp, evd = emit_dt_pipeline(c, "f", w, fr["f"]["dt"])
        ev_z = emit_zsilu(c, w, fr["f"]["evac"]) if full_tile else []
        if packing:
            x1o = (c0 - win) * CH
            pk = [p.op("Q", lambda q, x1o=x1o, w=w: q.dma_start(
                out=T["pack_dt"][:, x1o:x1o + w * 128],
                in_=sb["dtraw_b"][:, 0:w * 128]),
                after=fr["b"]["evac"] + prev_pack, sem="QD", dma=True)]
            prev_pack = pk

        def attn_mk(cl):
            lc = c0 + cl
            if lc < win:
                return None
            x1o = (lc - win) * CH
            rp_evs = fr["rp"][cl]

            def factory():
                smb = (cl % 4) * 32

                def emit_attn(mm_o, ev_r, B, rstd):
                    par = B // 4
                    ast_b = sb["attn_st" + ("2" if par else "")]
                    war = [aprev[par]] if aprev[par] else []
                    eqx = p.op("Q", lambda q: q.dma_start(
                        out=ast_b[:], in_=T["xc"][lc * CH:(lc + 1) * CH, :]),
                        after=war, sem="QD", dma=True)
                    er1 = p.op("V", lambda v: v.tensor_scalar(
                        ast_b[:], ast_b[:],
                        sb["sm"][:, smb + 14:smb + 15],
                        sb["sm"][:, smb + 15:smb + 16],
                        Alu.mult, Alu.subtract), after=rp_evs + [eqx])
                    er2 = p.op("V", lambda v: v.tensor_add(
                        ast_b[:], ast_b[:], sb["opb_bcast"][:]),
                        after=[er1])
                    e1 = p.rd("V", lambda v: v.scalar_tensor_tensor(
                        ast_b[:, 0:512], BK(PSUM, B + 0, 0, 512),
                        rstd, ast_b[:, 0:512],
                        Alu.mult, Alu.add), [B + 0],
                        after=mm_o + [ev_r, er2])
                    e2 = p.rd("V", lambda v: v.scalar_tensor_tensor(
                        ast_b[:, 512:1024], BK(PSUM, B + 1, 0, 512),
                        rstd, ast_b[:, 512:1024],
                        Alu.mult, Alu.add), [B + 1],
                        after=mm_o + [ev_r, er2])
                    ed = p.op("Q", lambda q: q.dma_start(
                        out=T["attn_f"][x1o:x1o + CH, :], in_=ast_b[:]),
                        after=[e1, e2], sem="QD", dma=True)
                    aprev[par] = ed
                    return [ed]
                return emit_attn
            return factory

        tails = run_pairs(c, st, "f", list(range(w)), evd, ev_sp, ev_conv,
                          ev_z, lambda cl: c0 + cl >= win, attn_mk, xbase=4)
        prev_tail = prev_tail_new
        chunk_tail = []
        for bk in tails:
            chunk_tail.append(bk["ev_state"])
            chunk_tail += bk["attn"]
        prev_consumers = chunk_tail + [prev_tail] + prev_pack


def emit_A2(c):
    nc, p, g, T, sb, PSUM = c["nc"], c["p"], c["g"], c["T"], c["sb"], c["PSUM"]
    win, span = g["win"], g["span"]
    ev_wf = p.op("Q", lambda q: q.dma_start(out=sb["Wf"][:],
                                            in_=T["wf_b"][:]),
                 sem="QD", dma=True)
    c["wf_ev"] = [ev_wf]
    p.op("V", lambda v: v.memset(sb["S32"][:], 0.0))
    e0d = p.op("G", lambda g_: g_.memset(sb["Sb"][:], 0.0))
    e0d2 = p.op("G", lambda g_: g_.memset(sb["Sb2"][:], 0.0))
    e0e = p.op("V", lambda v: v.memset(sb["xbc_pre"][:], 0.0))
    e0e3 = p.op("V", lambda v: v.memset(sb["xbc_pre3"][:], 0.0))
    st = dict(seq=0, sb_ready=e0d, sb_war=e0d2, stk={0: None, 1: None})
    ast = dict(prev_v=[], prev_ed=None, x0_done=[],
               prev_ed_par={0: None, 1: None})
    prev_tail = e0e
    prev_cons_t = {"": [], "3": [e0e3]}
    prev_consumers = []
    tiles = [(span - win, span)] + list(reversed(_tiles(win, span - win)))
    first = True
    for ti, (c0, c1) in enumerate(tiles):
        ts = "" if ti % 2 == 0 else "3"
        c["tsfx"] = ts
        c["tsfx_next"] = "3" if ts == "" else ""
        w = c1 - c0
        is_win = first
        first = False
        doff = 512 - w * 128
        if is_win:
            fr = emit_tile_front(c, c0, w, [("b", "xbcdt", None,
                                             sb["xbc_pre"], doff)],
                                 prev_chunk_consumers=prev_consumers)
            src_evs = fr["b"]["evac"] + [prev_tail]
            ev_dtraw = fr["b"]["dt"]
            ev_z = []
        else:
            x1o = (c0 - win) * CH
            pc = list(prev_cons_t[ts])
            e1 = p.op("S", lambda q, x1o=x1o, w=w, doff=doff, ts=ts:
                      q.dma_start(
                out=sb["xbc_pre" + ts].rearrange("p (b s) -> p b s", b=10)
                [:, :, doff:doff + w * 128],
                in_=T["pack_xbc"][:, x1o:x1o + w * 128]
                .rearrange("(c p) t -> p c t", p=128)),
                after=pc, sem="SD", dma=True)
            e2 = p.op("S", lambda q, x1o=x1o, w=w, ts=ts: q.dma_start(
                out=sb["zf" + ts].rearrange("p (b s) -> p b s", b=8)
                [:, :, 0:w * 128],
                in_=T["pack_z"][:, x1o:x1o + w * 128]
                .rearrange("(c p) t -> p c t", p=128)),
                after=pc, sem="SD", dma=True)
            e3 = p.op("S", lambda q, x1o=x1o, w=w, ts=ts: q.dma_start(
                out=sb["dtraw" + ts][:, 0:w * 128],
                in_=T["pack_dt"][:, x1o:x1o + w * 128]),
                after=pc, sem="SD", dma=True)
            src_evs = [e1, prev_tail]
            ev_dtraw = e3
            ev_z = [e2]
        ev_conv, prev_tail_new, _doff = emit_conv_silu(c, "b", w, src_evs)
        ev_sp, evd = emit_dt_pipeline(c, "b", w, ev_dtraw)

        def attn_mk(cl):
            if is_win:
                return None
            lc = c0 + cl
            x1o = (lc - win) * CH

            def factory():
                eq2 = p.op("Q", lambda q, x1o=x1o: q.dma_start(
                    out=sb["attnf_in"][:], in_=T["attn_f"][x1o:x1o + CH, :]),
                    after=([ast["prev_ed"]] if ast["prev_ed"] else []) +
                    ast["prev_v"], sem="QD", dma=True)
                ev_add = eq2

                def emit_attn(mm_o, ev_r, B, rstd):
                    par = B // 4
                    ast_b = sb["attn_st" + ("2" if par else "")]
                    war = ([ast["prev_ed_par"][par]]
                           if ast["prev_ed_par"][par] else [])
                    e1 = p.rd("V", lambda v: v.scalar_tensor_tensor(
                        ast_b[:, 0:512], BK(PSUM, B + 0, 0, 512),
                        rstd, sb["attnf_in"][:, 0:512],
                        Alu.mult, Alu.add), [B + 0],
                        after=mm_o + [ev_r, ev_add] + war)
                    e2 = p.rd("V", lambda v: v.scalar_tensor_tensor(
                        ast_b[:, 512:1024], BK(PSUM, B + 1, 0, 512),
                        rstd, sb["attnf_in"][:, 512:1024],
                        Alu.mult, Alu.add), [B + 1],
                        after=mm_o + [ev_r, ev_add] + war)
                    ed = p.op("Q", lambda q: q.dma_start(
                        out=T["x1s"][x1o:x1o + CH, :], in_=ast_b[:]),
                        after=[e1, e2], sem="QD", dma=True)
                    ast["prev_v"] = [e1, e2]
                    ast["prev_ed"] = ed
                    ast["prev_ed_par"][par] = ed
                    return [ed]
                return emit_attn
            return factory

        tails = run_pairs(c, st, "b", list(reversed(range(w))), evd, ev_sp,
                          ev_conv, ev_z, lambda cl: not is_win, attn_mk,
                          xbase=_doff)
        prev_tail = prev_tail_new
        chunk_tail = []
        for bk in tails:
            chunk_tail.append(bk["ev_state"])
            chunk_tail += bk["attn"]
        prev_cons_t[ts] = chunk_tail + [prev_tail]
        prev_consumers = chunk_tail + [prev_tail]



def emit_B(c):
    nc, p, g, T, sb, PSUM, PSUMb = (c["nc"], c["p"], c["g"], c["T"], c["sb"],
                                    c["PSUM"], c["PSUMb"])
    asb, mark, reset = c["asb"], c["mark"], c["reset"]
    nxc = g["x1_chunks"]

    mark()
    w1q = asb("w1q", [128, 8 * DHID], BF16)
    ring = asb("ring", [128, 32 * 1024], BF16)
    x1t = asb("b_x1", [128, 4 * DM], F32)
    x1q = asb("b_x1q", [128, 8 * 512], BF16)
    u_s = asb("b_u", [128, DM], F32)
    xq_t = asb("b_xqt", [128, DM], BF16)
    h2w = asb("b_h2w", [128, 32 * 512], BF16)
    accV = asb("b_accV", [128, 512], BF16)
    accG = asb("b_accG", [128, 512], BF16)
    tre4 = [asb(f"b_tree{i}", [128, 512], BF16) for i in range(4)]
    tre = tre4[0]
    accV2 = asb("b_accV2", [128, 512], BF16)
    accG2 = asb("b_accG2", [128, 512], BF16)
    s2bc = asb("b_s2bc", [128, 512], BF16)
    s2tm = asb("b_s2tm", [128, 4], F32)
    stash = asb("b_stash", [128, 32], BF16)
    s2row = asb("b_s2row", [1, 512], F32)
    fcw = asb("b_fcw", [128, 32 * 3], F32)
    fcb = asb("b_fcb", [128, 32], F32)
    wsum = asb("b_wsum", [128, 1], F32)
    wtmp = asb("b_wtmp", [128, 1], F32)
    sw1 = asb("b_sw1", [128, 1], F32)
    sw1i = asb("b_sw1i", [128, 1], F32)
    bsm = asb("b_bsm", [128, 4 * 32], F32)
    hmask = asb("b_hm", [128, 4], F32)
    junk = asb("b_junk", [128, DM], BF16)
    wld = asb("b_wld", [128, DHID], F32,
              at=int(h2w.tensor.manual_sbuf_range[0]))

    def ld(dst, src, after=()):
        return p.op("Q", lambda q: q.dma_start(
            out=dst, in_=src, allow_slow_non_contiguous=True), after=after,
            sem="QD", dma=True)

    ld(fcw[:].rearrange("p (c k) -> p c k", c=32),
       T["fconv_w"].rearrange("c p k -> p c k"))
    ld(fcb[:].rearrange("p (c o) -> p c o", c=32),
       T["fconv_b"].rearrange("c p o -> p c o"))
    e_rm = p.op("V", lambda v: v.memset(ring[:], 0.0))

    ev_acc = None
    prev_ld = None
    for cc in range(8):
        e1 = ld(wld[:], T["w1T"].rearrange("(c p) j -> p c j", p=128)
                [:, cc, :], after=[prev_ld] if prev_ld else [])
        e2 = p.op("V", lambda v: v.reduce_sum(wtmp[:], wld[:], axis=AX.X,
                                              apply_absolute_value=True),
                  after=[e1] + ([ev_acc] if ev_acc else []))
        prev_ld = e2
        if cc == 0:
            ev_acc = p.op("V", lambda v: v.tensor_copy(wsum[:], wtmp[:]),
                          after=[e2])
        else:
            ev_acc = p.op("V", lambda v: v.tensor_add(wsum[:], wsum[:], wtmp[:]),
                          after=[e2])
    e3 = p.mm(lambda t: t.matmul(PSUM[0:1, 0:1], wsum[:], sb["ones_col"][:],
                                 start=True, stop=True), [0],
              after=[ev_acc], opens=True)
    e4 = p.rd("V", lambda v: v.tensor_scalar_mul(wtmp[0:1, 0:1],
                                                 PSUM[0:1, 0:1],
                                                 1.0 / (DHID * DM)),
              [0], after=[e3])
    e5 = p.op("V", lambda v: v.tensor_scalar_max(wtmp[0:1, 0:1], wtmp[0:1, 0:1],
                                                 1e-5), after=[e4])
    e6 = p.mm(lambda t: t.matmul(BK(PSUM, 1, 0, 1), sb["ones_row"][:],
                                 wtmp[0:1, 0:1], start=True, stop=True), [1],
              after=[e5], opens=True)
    e7 = p.rd("V", lambda v: v.tensor_copy(sw1i[:], BK(PSUM, 1, 0, 1)), [1],
              after=[e6])
    e8 = p.op("V", lambda v: v.reciprocal(sw1[:], sw1i[:]), after=[e7])
    ev_w1 = []
    for cc in range(8):
        e1 = ld(wld[:], T["w1T"].rearrange("(c p) j -> p c j", p=128)
                [:, cc, :], after=[prev_ld] + (ev_w1[-1:] if ev_w1 else []))
        e2 = p.op("V", lambda v: v.tensor_scalar(wld[:], wld[:], sw1[:], MAGIC,
                                                 Alu.mult, Alu.add),
                  after=[e1, e8])
        e2b = p.op("V", lambda v: v.tensor_scalar(wld[:], wld[:], MAGIC, 1.0,
                                                  Alu.subtract, Alu.min),
                   after=[e2])
        ev_w1.append(p.op("V", lambda v, cc=cc: v.tensor_scalar(
            w1q[:, cc * DHID:(cc + 1) * DHID], wld[:], -1.0, 0.0,
            Alu.max, Alu.add), after=[e2b]))
        prev_ld = ev_w1[-1]

    env = dict(w1q=w1q, ring=ring, x1t=x1t, x1q=x1q, u_s=u_s, xq_t=xq_t,
               h2w=h2w, accV=accV, accG=accG, tre=tre, s2bc=s2bc, s2tm=s2tm,
               fcw=fcw, fcb=fcb, sw1i=sw1i, bsm=bsm, hmask=hmask, junk=junk,
               stash=stash, s2row=s2row, tre4=tre4,
               accV2=accV2, accG2=accG2)
    e_sm = p.op("V", lambda v: v.memset(stash[:], 0.0))

    tiles = _tiles(0, nxc)
    ev_ring_t = []
    win_evs = {}

    def emit_front(ti, prev_front, prev_win, mm_war):
        (c0, c1) = tiles[ti]
        w = c1 - c0
        eq = ld(x1t[:, 0:w * DM].rearrange("p (c d) -> p c d", c=w),
                T["x1s"][c0 * CH:c1 * CH, :].rearrange("(c p) d -> p c d", p=128),
                after=prev_front)
        em = p.op("Q", lambda q, c0=c0, c1=c1, w=w: q.dma_start(
            out=hmask[:, 0:w].rearrange("p (c o) -> p c o", c=w),
            in_=T["halo_mask"][c0 * CH:c1 * CH, :]
            .rearrange("(c p) o -> p c o", p=128),
            allow_slow_non_contiguous=True), after=prev_front,
            sem="QD", dma=True)
        ev_q = []
        for cc in range(w):
            x1c = x1t[:, cc * DM:(cc + 1) * DM]
            smb = (cc % 4) * 32
            B_ = bsm

            def S(i, smb=smb):
                return B_[:, smb + i:smb + i + 1]
            b1e = p.op("V", lambda v, x1c=x1c, smb=smb: v.bn_stats(
                B_[:, smb + 16:smb + 22], x1c[:, 0:512]),
                after=[eq] + prev_win)
            b2e = p.op("V", lambda v, x1c=x1c, smb=smb: v.bn_stats(
                B_[:, smb + 22:smb + 28], x1c[:, 512:1024]),
                after=[eq] + prev_win)
            e1 = p.op("V", lambda v, smb=smb: v.bn_aggr(
                B_[:, smb + 4:smb + 6], B_[:, smb + 16:smb + 28]),
                after=[b1e, b2e])
            e5 = p.op("V", lambda v, smb=smb: v.tensor_scalar_add(
                S(6, smb), S(5, smb), EPS), after=[e1])
            e6 = p.op("S", lambda s, smb=smb: s.activation(
                S(7, smb), S(6, smb), Act.Ln), after=[e5])
            e7 = p.op("S", lambda s, smb=smb: s.activation(
                S(8, smb), S(7, smb), Act.Exp, scale=-0.5), after=[e6])
            e7b = p.op("V", lambda v, cc=cc, smb=smb: v.tensor_mul(
                S(8, smb), S(8, smb), hmask[:, cc:cc + 1]), after=[e7, em])
            e9 = p.op("V", lambda v, x1c=x1c, smb=smb: v.tensor_scalar(
                u_s[:], x1c, S(4, smb), S(8, smb), Alu.subtract, Alu.mult),
                after=[e7b])
            e10 = p.op("V", lambda v, smb=smb: v.reduce_max(
                S(9, smb), u_s[:], axis=AX.X, apply_absolute_value=True),
                after=[e9])
            e11 = p.op("V", lambda v, smb=smb: v.tensor_scalar_max(
                S(10, smb), S(9, smb), 1e-5), after=[e10])
            e12 = p.op("V", lambda v, smb=smb: v.reciprocal(
                S(11, smb), S(10, smb)), after=[e11])
            e13 = p.op("V", lambda v, smb=smb: v.tensor_scalar_mul(
                S(12, smb), S(11, smb), 127.0), after=[e12])
            e14 = p.op("V", lambda v, smb=smb: v.tensor_mul(
                S(13, smb), S(10, smb), sw1i[:]), after=[e11, e8])
            e15 = p.op("V", lambda v, smb=smb: v.tensor_scalar_mul(
                S(14, smb), S(13, smb), 1.0 / 127.0), after=[e14])
            e16 = p.op("V", lambda v, smb=smb: v.tensor_scalar(
                u_s[:], u_s[:], S(12, smb), MAGIC, Alu.mult, Alu.add),
                after=[e13, e9])
            e17 = p.op("V", lambda v, smb=smb: v.tensor_scalar(
                xq_t[:], u_s[:], MAGIC, S(14, smb), Alu.subtract, Alu.mult),
                after=[e16, e15])
            for b8 in range(8):
                bk = 6 + b8 % 2
                et = p.mm(lambda t, b8=b8, bk=bk: t.transpose(
                    PSUMb[:, bk * 1024:bk * 1024 + 128],
                    xq_t[:, b8 * 128:b8 * 128 + 128], sb["I128b"][:]),
                    [bk], after=[e17], opens=True)
                ev_q.append(p.rd("V", lambda v, b8=b8, bk=bk, cc=cc:
                                 v.tensor_copy(
                                     x1q[:, b8 * 512 + cc * 128:
                                         b8 * 512 + cc * 128 + 128],
                                     PSUMb[:, bk * 1024:bk * 1024 + 128]),
                                 [bk], after=[et] + mm_war))
        return ev_q

    fronts = {0: emit_front(0, [], [e_sm], [])}
    for ti, (c0, c1) in enumerate(tiles):
        w = c1 - c0
        ev_q = fronts[ti]
        rslot = (ti % 2) * 512
        ev_ring = []
        mm_last = None
        for j in range(32):
            bk = j % 4
            last = None
            for ccn in range(8):
                def mmf(t, j=j, ccn=ccn, bk=bk, w=w):
                    return t.matmul(
                        BK(PSUM, bk, 0, w * 128),
                        w1q[:, ccn * DHID + j * 128:ccn * DHID + j * 128 + 128],
                        x1q[:, ccn * 512:ccn * 512 + w * 128],
                        start=(ccn == 0), stop=(ccn == 7))
                last = p.mm(mmf, [bk], after=ev_q + [ev_w1[-1], e_rm],
                            opens=(ccn == 0))
            mm_last = last
            guard = win_evs.get(ti - 2, [e_sm, e_rm])
            ev_ring.append(p.rd("S", lambda s, j=j, bk=bk, w=w, rslot=rslot:
                                s.activation(
                                    ring[:, j * 1024 + rslot:j * 1024 + rslot + w * 128],
                                    BK(PSUM, bk, 0, w * 128), Act.Copy),
                                [bk], after=[last] + guard))
        ev_ring_t.append(ev_ring)
        if ti + 1 < len(tiles):
            fronts[ti + 1] = emit_front(ti + 1, ev_q,
                                        win_evs.get(ti - 1, [e_sm]),
                                        [mm_last])
        if ti > 0:
            win_evs[ti - 1] = emit_B1_window(c, env, tiles, ti - 1, ev_ring_t,
                                             win_evs.get(ti - 2, [e_sm]))
    win_evs[len(tiles) - 1] = emit_B1_window(c, env, tiles, len(tiles) - 1,
                                             ev_ring_t,
                                             win_evs.get(len(tiles) - 2, [e_sm]))
    p.barrier()
    reset()

    mark()
    w2q = asb("w2q", [128, 32 * DM], BF16)
    h2in = asb("b2_h2", [128, 2 * 32 * 512], BF16)
    x1r = asb("b2_x1", [128, 4 * DM], F32)
    outst = asb("b2_out", [128, 4 * DM], F32)
    s2t = asb("b2_s2", [128, 4], F32)
    w2sum = asb("b2_wsum", [128, 1], F32)
    w2tmp = asb("b2_wtmp", [128, 1], F32)
    sw2m = asb("b2_sw2m", [128, 1], F32)
    w2ld = asb("b2_wld", [128, DM], F32)

    ev_acc = None
    prev_ld = None
    for cc in range(32):
        e1 = ld(w2ld[:], T["w2T"].rearrange("(c p) j -> p c j", p=128)
                [:, cc, :], after=[prev_ld] if prev_ld else [])
        e2 = p.op("V", lambda v: v.reduce_sum(w2tmp[:], w2ld[:], axis=AX.X,
                                              apply_absolute_value=True),
                  after=[e1] + ([ev_acc] if ev_acc else []))
        prev_ld = e2
        if cc == 0:
            ev_acc = p.op("V", lambda v: v.tensor_copy(w2sum[:], w2tmp[:]),
                          after=[e2])
        else:
            ev_acc = p.op("V", lambda v: v.tensor_add(w2sum[:], w2sum[:],
                                                      w2tmp[:]), after=[e2])
    e3 = p.mm(lambda t: t.matmul(PSUM[0:1, 0:1], w2sum[:], sb["ones_col"][:],
                                 start=True, stop=True), [0], after=[ev_acc],
              opens=True)
    e4 = p.rd("V", lambda v: v.tensor_scalar_mul(w2tmp[0:1, 0:1],
                                                 PSUM[0:1, 0:1],
                                                 1.0 / (DHID * DM)),
              [0], after=[e3])
    e5 = p.op("V", lambda v: v.tensor_scalar_max(w2tmp[0:1, 0:1],
                                                 w2tmp[0:1, 0:1], 1e-5),
              after=[e4])
    e6 = p.mm(lambda t: t.matmul(BK(PSUM, 1, 0, 1), sb["ones_row"][:],
                                 w2tmp[0:1, 0:1], start=True, stop=True), [1],
              after=[e5], opens=True)
    e7 = p.rd("V", lambda v: v.tensor_copy(sw2m[:], BK(PSUM, 1, 0, 1)), [1],
              after=[e6])
    e8v = p.op("V", lambda v: v.reciprocal(w2tmp[:], sw2m[:]), after=[e7])
    ev_w2 = []
    for cc in range(32):
        e1 = ld(w2ld[:], T["w2T"].rearrange("(c p) j -> p c j", p=128)
                [:, cc, :], after=[prev_ld] + (ev_w2[-1:] if ev_w2 else []))
        e2 = p.op("V", lambda v: v.tensor_scalar(w2ld[:], w2ld[:], w2tmp[:],
                                                 MAGIC, Alu.mult, Alu.add),
                  after=[e1, e8v])
        e2b = p.op("V", lambda v: v.tensor_scalar(w2ld[:], w2ld[:], MAGIC, 1.0,
                                                  Alu.subtract, Alu.min),
                   after=[e2])
        ev_w2.append(p.op("V", lambda v, cc=cc: v.tensor_scalar(
            w2q[:, cc * DM:(cc + 1) * DM], w2ld[:], -1.0, 0.0, Alu.max,
            Alu.add), after=[e2b]))
        prev_ld = ev_w2[-1]

    kept = g["kept"]
    prev_out = []
    prev_loads = []
    for k in range(kept // 4):
        t0 = CH + 512 * k
        par = k % 2
        eh = ld(h2in[:, par * 32 * 512:(par + 1) * 32 * 512]
                .rearrange("p (c t) -> p c t", c=32),
                T["h2qs"][:, t0:t0 + 512].rearrange("(c p) t -> p c t", p=128),
                after=prev_loads[-2:] if len(prev_loads) >= 2 else [])
        ex = ld(x1r[:].rearrange("p (c d) -> p c d", c=4),
                T["x1s"][t0:t0 + 512, :]
                .rearrange("(c p) d -> p c d", p=128), after=prev_out)
        es = p.op("Q", lambda q, t0=t0: q.dma_start(
            out=s2t[:, 0:4].rearrange("p (c o) -> p c o", c=4),
            in_=T["s2fs"][t0:t0 + 512, :].rearrange("(c p) o -> p c o", p=128),
            allow_slow_non_contiguous=True), after=prev_out,
            sem="QD", dma=True)
        ea = p.op("V", lambda v: v.tensor_scalar_mul(s2t[:, 0:4], s2t[:, 0:4],
                                                     sw2m[:]), after=[es, e7])
        new_out = []
        for tc in range(4):
            for half in range(2):
                bk = (tc * 2 + half) % 8
                last = None
                for hc in range(32):
                    def mmf(t, tc=tc, half=half, hc=hc, bk=bk, par=par):
                        return t.matmul(
                            BK(PSUM, bk, 0, 512),
                            h2in[:, par * 32 * 512 + hc * 512 + tc * 128:
                                 par * 32 * 512 + hc * 512 + tc * 128 + 128],
                            w2q[:, hc * DM + half * 512:hc * DM + half * 512 + 512],
                            start=(hc == 0), stop=(hc == 31))
                    last = p.mm(mmf, [bk], after=[eh, ev_w2[-1]],
                                opens=(hc == 0))
                new_out.append(p.rd("V", lambda v, tc=tc, half=half, bk=bk:
                                    v.scalar_tensor_tensor(
                                        outst[:, tc * DM + half * 512:
                                              tc * DM + half * 512 + 512],
                                        BK(PSUM, bk, 0, 512), s2t[:, tc:tc + 1],
                                        x1r[:, tc * DM + half * 512:
                                            tc * DM + half * 512 + 512],
                                        Alu.mult, Alu.add), [bk],
                                    after=[last, ex, ea]))
        prev_out = [p.op("Q", lambda q, k=k: q.dma_start(
            out=T["y"][512 * k:512 * k + 512, :]
            .rearrange("(c p) d -> p c d", p=128),
            in_=outst[:].rearrange("p (c d) -> p c d", c=4)),
            after=new_out, sem="QD", dma=True)]
        prev_loads.append(eh)


def emit_B1_window(c, env, tiles, wi, ev_ring_t, prev_stash_rd):
    p, sb, PSUM, PSUMb, T = c["p"], c["sb"], c["PSUM"], c["PSUMb"], c["T"]
    ring, fcw, fcb = env["ring"], env["fcw"], env["fcb"]
    accV, accG, tre = env["accV"], env["accG"], env["tre"]
    tre4 = env["tre4"]
    h2w, s2bc, s2tm, bsm = env["h2w"], env["s2bc"], env["s2tm"], env["bsm"]
    stash, s2row = env["stash"], env["s2row"]
    sw1i = env["sw1i"]
    c0, c1 = tiles[wi]
    w = c1 - c0
    rslot = (wi % 2) * 512
    after0 = list(ev_ring_t[wi])
    if wi + 1 < len(tiles):
        after0 += ev_ring_t[wi + 1]
    nwtok = w * 128
    after0 = after0 + list(prev_stash_rd)
    ev_sil = []
    tree_evs = [None, None, None, None]
    last_e = {0: None, 1: None, 2: None, 3: None}
    ACCN = [accV, accG, env["accV2"], env["accG2"]]
    for blk in range(32):
        eng = "V"
        acc = ACCN[blk % 4]
        base = blk * 1024
        pe = last_e[blk % 4]
        e = p.op(eng, lambda v, blk=blk, base=base, acc=acc: v.tensor_scalar(
            acc[:, 0:nwtok], ring[:, base + rslot:base + rslot + nwtok],
            fcw[:, blk * 3 + 1:blk * 3 + 2], fcb[:, blk:blk + 1],
            Alu.mult, Alu.add), after=after0 + ([pe] if pe else []))
        e = p.op(eng, lambda v, blk=blk, acc=acc: v.scalar_tensor_tensor(
            acc[:, 0:1], stash[:, blk:blk + 1],
            fcw[:, blk * 3 + 0:blk * 3 + 1], acc[:, 0:1],
            Alu.mult, Alu.add), after=[e])
        e = p.op(eng, lambda v, blk=blk, base=base, acc=acc:
                 v.scalar_tensor_tensor(
            acc[:, 1:nwtok], ring[:, base + rslot:base + rslot + nwtok - 1],
            fcw[:, blk * 3 + 0:blk * 3 + 1], acc[:, 1:nwtok],
            Alu.mult, Alu.add), after=[e])
        pieces = []
        tlo = 0
        while tlo < nwtok:
            col = (rslot + 1 + tlo) % 1024
            run = min(nwtok - tlo, 1024 - col)
            pieces.append((tlo, col, run))
            tlo += run
        for (tlo, col, run) in pieces:
            e = p.op(eng, lambda v, blk=blk, base=base, col=col, run=run,
                     tlo=tlo, acc=acc: v.scalar_tensor_tensor(
                         acc[:, tlo:tlo + run],
                         ring[:, base + col:base + col + run],
                         fcw[:, blk * 3 + 2:blk * 3 + 3],
                         acc[:, tlo:tlo + run], Alu.mult, Alu.add),
                     after=[e])
        es = p.op("S", lambda s, blk=blk, nwtok=nwtok, acc=acc: s.activation(
            h2w[:, blk * 512:blk * 512 + nwtok], acc[:, 0:nwtok], Act.Silu),
            after=[e])
        last_e[blk % 4] = es
        ev_sil.append(es)
        lane = blk % 4
        trl = tre4[lane]
        if tree_evs[lane] is None:
            tree_evs[lane] = p.op("V", lambda v, nwtok=nwtok, blk=blk, trl=trl:
                                  v.scalar_tensor_tensor(
                trl[:, 0:nwtok], h2w[:, blk * 512:blk * 512 + nwtok], -1.0,
                h2w[:, blk * 512:blk * 512 + nwtok],
                Alu.mult, Alu.max), after=[es])
        else:
            t_ev = p.op("V", lambda v, blk=blk, nwtok=nwtok, trl=trl:
                        v.tensor_max(
                trl[:, 0:nwtok], trl[:, 0:nwtok],
                h2w[:, blk * 512:blk * 512 + nwtok]),
                after=[es, tree_evs[lane]])
            tree_evs[lane] = p.op("V", lambda v, blk=blk, nwtok=nwtok, trl=trl:
                                  v.scalar_tensor_tensor(
                trl[:, 0:nwtok], h2w[:, blk * 512:blk * 512 + nwtok], -1.0,
                trl[:, 0:nwtok], Alu.mult, Alu.max), after=[t_ev])
    ev_stash = p.op("V", lambda v: v.tensor_copy(
        stash[:].unsqueeze(2),
        ring.rearrange("p (b s) -> p b s", b=32)
        [:, :, rslot + nwtok - 1:rslot + nwtok].copy() if False else
        ring.rearrange("p (b s) -> p b s", b=32)[:, :, rslot + nwtok - 1:rslot + nwtok]),
        after=ev_sil + prev_stash_rd)
    e_c1 = p.op("V", lambda v, nwtok=nwtok: v.tensor_max(
        tre4[0][:, 0:nwtok], tre4[0][:, 0:nwtok], tre4[1][:, 0:nwtok]),
        after=[tree_evs[0], tree_evs[1]])
    e_c2 = p.op("V", lambda v, nwtok=nwtok: v.tensor_max(
        tre4[2][:, 0:nwtok], tre4[2][:, 0:nwtok], tre4[3][:, 0:nwtok]),
        after=[tree_evs[2], tree_evs[3]])
    tree_ev = p.op("V", lambda v, nwtok=nwtok: v.tensor_max(
        tre4[0][:, 0:nwtok], tre4[0][:, 0:nwtok], tre4[2][:, 0:nwtok]),
        after=[e_c1, e_c2])
    s2_evs = []
    prev = [tree_ev]
    for cc in range(w):
        et = p.mm(lambda t, cc=cc: t.transpose(
            PSUMb[:, 6 * 1024:6 * 1024 + 128],
            tre[:, cc * 128:cc * 128 + 128], sb["I128b"][:]), [6],
            after=prev, opens=True)
        e1 = p.rd("V", lambda v: v.reduce_max(bsm[:, 0:1],
                                              PSUMb[:, 6 * 1024:6 * 1024 + 128],
                                              axis=AX.X), [6], after=[et])
        e2 = p.op("V", lambda v: v.tensor_scalar_max(bsm[:, 1:2], bsm[:, 0:1],
                                                     1e-5), after=[e1])
        e4 = p.op("V", lambda v: v.reciprocal(bsm[:, 2:3], bsm[:, 1:2]),
                  after=[e2])
        e4b = p.op("V", lambda v: v.tensor_scalar_mul(bsm[:, 3:4], bsm[:, 2:3],
                                                      127.0), after=[e4])
        e5 = p.op("V", lambda v: v.tensor_scalar_mul(s2tm[:, 0:1], bsm[:, 1:2],
                                                     1.0 / 127.0), after=[e2])
        eqs = p.op("Q", lambda q, cc=cc, c0=c0: q.dma_start(
            out=T["s2fs"][(c0 + cc) * CH:(c0 + cc + 1) * CH, :],
            in_=s2tm[:, 0:1]), after=[e5], sem="QD", dma=True)
        et2 = p.mm(lambda t: t.transpose(PSUM[0:1, 6 * 512:6 * 512 + 128],
                                         bsm[:, 3:4], sb["I128f"][:]), [6],
                   after=[e4b, e1], opens=True)
        e6 = p.rd("V", lambda v, cc=cc: v.tensor_copy(
            s2row[0:1, cc * 128:cc * 128 + 128],
            PSUM[0:1, 6 * 512:6 * 512 + 128]), [6], after=[et2])
        s2_evs.append(e6)
        prev = [e6, eqs]
    e7 = p.mm(lambda t: t.matmul(BK(PSUM, 7, 0, nwtok), sb["ones_row"][:],
                                 s2row[0:1, 0:nwtok], start=True, stop=True),
              [7], after=s2_evs, opens=True)
    e8 = p.rd("V", lambda v: v.tensor_copy(s2bc[:, 0:nwtok],
                                           BK(PSUM, 7, 0, nwtok)), [7],
              after=[e7])
    q_evs = []
    for blk in range(32):
        e1 = p.op("V", lambda v, blk=blk: v.tensor_tensor(
            h2w[:, blk * 512:blk * 512 + nwtok],
            h2w[:, blk * 512:blk * 512 + nwtok], s2bc[:, 0:nwtok], Alu.mult),
            after=[e8, ev_sil[blk], ev_stash])
        q_evs.append(p.op("V", lambda v, blk=blk: v.tensor_scalar(
            h2w[:, blk * 512:blk * 512 + nwtok],
            h2w[:, blk * 512:blk * 512 + nwtok], MAGIC, MAGIC,
            Alu.add, Alu.subtract), after=[e1]))
    ed = p.op("Q", lambda q: q.dma_start(
        out=T["h2qs"][:, c0 * CH:c0 * CH + nwtok]
        .rearrange("(c p) t -> p c t", p=128),
        in_=h2w.rearrange("p (b s) -> p b s", b=32)[:, :, 0:nwtok]),
        after=q_evs, sem="QD", dma=True)
    return q_evs + [ed, ev_stash, e8] + prev




_NC_CACHE = {}


def _nc_for(gkey):
    if gkey not in _NC_CACHE:
        _NC_CACHE[gkey] = build_nc(geom(*gkey))
    return _NC_CACHE[gkey]


def _consts(g):
    import ml_dtypes
    bf = ml_dtypes.bfloat16
    r = np.arange(128)
    triu = (r[:, None] <= r[None, :]).astype(np.float32)
    tril = triu.T.copy()
    triu_s = (r[:, None] < r[None, :]).astype(np.float32)
    tril_s = triu_s.T.copy()
    TRIS = np.concatenate([triu, tril, tril_s, triu_s], axis=1)
    MASKS = np.concatenate([triu, tril], axis=1).astype(bf)
    SELB = np.zeros((NH, 8 * 128), np.float32)
    for ci in range(8):
        for pp in range(128):
            SELB[2 * ci + pp // 64, ci * 128 + pp] = 1.0
    SELB = SELB.astype(bf)
    return dict(
        I128b=np.eye(128).astype(bf), I128f=np.eye(128, dtype=np.float32),
        MASKS=np.ascontiguousarray(MASKS),
        SELB=SELB, ones_col=np.ones((128, 1), np.float32),
        ones_row=np.ones((1, 128), np.float32))


def _weight_inputs(inputs, g):
    f32 = np.float32
    w = {}
    for d, pre in (("f", "L0_attn_f_"), ("b", "L0_attn_b_")):
        inw = np.asarray(inputs[pre + "in_w"], f32)
        wT = np.zeros((DH, JP), f32)
        wT[:, :2320] = np.ascontiguousarray(inw.T)
        w[f"in_wT_{d}"] = wT
        w[f"conv_w_{d}"] = np.ascontiguousarray(
            np.asarray(inputs[pre + "conv_w"], f32).reshape(10, 128, 4))
        w[f"conv_b_{d}"] = np.ascontiguousarray(
            np.asarray(inputs[pre + "conv_b"], f32).reshape(10, 128, 1))
        w[f"dt_bias_{d}"] = np.asarray(inputs[pre + "dt_bias"], f32).reshape(NH, 1)
        w[f"A_log_{d}"] = np.asarray(inputs[pre + "A_log"], f32).reshape(NH, 1)
        D = np.asarray(inputs[pre + "D"], f32)
        idx = 2 * np.arange(8)[None, :] + (np.arange(128)[:, None] // 64)
        w[f"D64_{d}"] = np.ascontiguousarray(D[idx]).astype(f32)
        w[f"outp_{d}"] = np.asarray(inputs[pre + "outp_w"], f32)
        w[f"gnorm_row_{d}"] = np.asarray(inputs[pre + "gnorm_w"], f32).reshape(1, DIN)
    w["norm_col"] = np.asarray(inputs["L0_attn_norm_w"], f32).reshape(DM, 1)
    w["op_wT"] = np.ascontiguousarray(np.asarray(inputs["L0_attn_op_w"], f32).T)
    w["opb_row"] = np.asarray(inputs["L0_attn_op_b"], f32).reshape(1, DM)
    w["w1T"] = np.ascontiguousarray(np.asarray(inputs["L0_ffn_w1"], f32).T)
    w["w2T"] = np.ascontiguousarray(np.asarray(inputs["L0_ffn_w2"], f32).T)
    w["fconv_w"] = np.ascontiguousarray(
        np.asarray(inputs["L0_ffn_conv_w"], f32).reshape(DHID, 3).reshape(32, 128, 3))
    w["fconv_b"] = np.ascontiguousarray(
        np.asarray(inputs["L0_ffn_conv_b"], f32).reshape(32, 128, 1))
    return w


def _core_inputs(xfull, b, s, e, L, g, wshared, consts):
    f32 = np.float32
    span_rows = g["span"] * CH
    x1_rows = g["x1_chunks"] * CH
    lo = s - (g["win"] + 1) * CH
    xc = np.zeros((span_rows, DM), f32)
    a = max(lo, 0)
    bnd = min(lo + span_rows, L)
    xc[a - lo:bnd - lo] = xfull[b, a:bnd]
    hm = np.ones((x1_rows, 1), f32)
    if s == 0:
        hm[127, 0] = 0.0
    if e == L:
        hm[(g["kept"] + 1) * CH, 0] = 0.0
    d = dict(xc=xc, halo_mask=hm)
    d.update(wshared)
    d.update(consts)
    return d


def kernel(**inputs):
    g = geom(32, 4)
    L = 16384
    x = np.asarray(inputs["L0_attn_x"], np.float32)
    nc = _nc_for((32, 4))
    wshared = _weight_inputs(inputs, g)
    consts = _consts(g)
    in_maps = []
    metas = []
    for core in range(8):
        b = core // 4
        s = (core % 4) * 4096
        e = s + 4096
        in_maps.append(_core_inputs(x, b, s, e, L, g, wshared, consts))
        metas.append((b, s, e))
    res = run_bass_kernel_spmd(nc, in_maps, core_ids=list(range(8)))
    out = np.empty((2, L, DM), np.float32)
    for core, (b, s, e) in enumerate(metas):
        out[b, s:e] = res.results[core]["y"]
    return out



# version 26
# speedup vs baseline: 1.0079x; 1.0079x over previous
import numpy as np
import concourse.bass as bass
import concourse.mybir as mybir
from concourse.bass_utils import run_bass_kernel_spmd

F32 = mybir.dt.float32
BF16 = mybir.dt.bfloat16
Alu = mybir.AluOpType
Act = mybir.ActivationFunctionType
AX = mybir.AxisListType

CH = 128
NH = 16
HD = 64
DIN = 1024
CD = 1280
DM = 1024
DH = 512
JP = 2432
NJ = 19
DHID = 4096
EPS = 1e-5
MAGIC = 12582912.0
XBS = 516


def geom(kept=32, win=4):
    return dict(kept=kept, win=win, span=kept + 2 + 2 * win,
                f_hi=win + 1 + kept + 1, x1_chunks=kept + 2,
                out_rows=kept * CH)


def _tiles(lo, hi, w=4):
    t, c0 = [], lo
    while c0 < hi:
        t.append((c0, min(c0 + w, hi)))
        c0 += w
    return t


class Prog:
    ENG = ("T", "V", "S", "G", "Q")

    def __init__(self, nc):
        self.nc = nc
        self.ops = {e: [] for e in self.ENG}
        self.cnt = {}
        self.order = []
        self.last_wait = {e: {} for e in self.ENG}
        self.bank_readers = {b: [] for b in range(8)}

    def sem(self, name):
        if name not in self.cnt:
            self.cnt[name] = 0
            self.order.append(name)

    def ev(self, s):
        self.sem(s)
        return (s, self.cnt[s])

    NDMA = 6

    def op(self, eng, fn, after=(), sem=None, dma=False):
        if sem is None:
            sem = eng
        after = list(after)
        if dma:
            if not hasattr(self, "dma_rot"):
                self.dma_rot = {}
            idx = self.dma_rot.get(sem, 0)
            self.dma_rot[sem] = idx + 1
            sem = f"{sem}{idx % self.NDMA}"
            self.sem(sem)
            if self.cnt[sem] > 0:
                after.append((sem, self.cnt[sem]))
        self.sem(sem)
        inc = 16 if dma else 1
        waits = []
        la = self.last_wait[eng]
        for a in after:
            if a is None:
                continue
            (s, v) = a
            if v <= 0 or la.get(s, 0) >= v:
                continue
            la[s] = v
            waits.append((s, v))
        self.ops[eng].append((fn, waits, sem, inc))
        self.cnt[sem] += inc
        return (sem, self.cnt[sem])

    def mm(self, fn, banks, after=(), opens=False):
        extra = []
        if opens:
            for b in banks:
                extra += self.bank_readers[b]
                self.bank_readers[b] = []
        return self.op("T", fn, after=list(after) + extra)

    def rd(self, eng, fn, banks, after=(), **kw):
        e = self.op(eng, fn, after=after, **kw)
        for b in banks:
            self.bank_readers[b].append(e)
        return e

    def barrier(self):
        evs = [self.ev(s) for s in list(self.order)]
        for e in self.ENG:
            self.op(e, lambda eng: eng.nop(), after=list(evs))

    def realize(self):
        nc = self.nc
        handles = {name: nc.alloc_semaphore(name) for name in self.order}
        attr = {"T": "tensor", "V": "vector", "S": "scalar",
                "G": "gpsimd", "Q": "sync"}
        with nc.Block() as block:
            for ename in self.ENG:
                def body(eng, _e=ename):
                    for (fn, waits, sem, inc) in self.ops[_e]:
                        for (s, v) in waits:
                            eng.wait_ge(handles[s], v)
                        instr = fn(eng)
                        if instr is not None:
                            instr.then_inc(handles[sem], inc)
                getattr(block, attr[ename])(body)


def BK(PSUM, b, lo=0, hi=512):
    return PSUM[:, 512 * b + lo:512 * b + hi]


def build_nc(g):
    nc = bass.Bass("TRN2", target_bir_lowering=False)
    span_rows = g["span"] * CH
    x1_rows = g["x1_chunks"] * CH
    T = {}

    def din(name, shape, dtype=F32):
        T[name] = nc.dram_tensor(name, shape, dtype, kind="ExternalInput").ap()

    import os
    _dbg = os.environ.get("K_DBG") == "1"

    def dint(name, shape, dtype=F32):
        kind = "ExternalOutput" if _dbg else "Internal"
        T[name] = nc.dram_tensor(name, shape, dtype, kind=kind).ap()

    din("xc", [span_rows, DM])
    din("in_wT_f", [DH, JP]); din("in_wT_b", [DH, JP])
    din("norm_col", [DM, 1])
    for d in ("f", "b"):
        din(f"conv_w_{d}", [10, 128, 4]); din(f"conv_b_{d}", [10, 128, 1])
        din(f"dt_bias_{d}", [NH, 1]); din(f"A_log_{d}", [NH, 1])
        din(f"D64_{d}", [CH, 8])
        din(f"outp_{d}", [DH, DIN])
        din(f"gnorm_row_{d}", [1, DIN])
    din("op_wT", [DM, DM])
    din("opb_row", [1, DM])
    din("w1T", [DM, DHID]); din("w2T", [DHID, DM])
    din("fconv_w", [32, 128, 3]); din("fconv_b", [32, 128, 1])
    din("halo_mask", [x1_rows, 1])
    din("I128b", [128, 128], BF16)
    din("I128f", [128, 128], F32)
    din("MASKS", [128, 2 * 128], BF16)
    din("SELB", [NH, 8 * 128], BF16)
    din("ones_col", [128, 1])
    din("ones_row", [1, 128])

    T["y"] = nc.dram_tensor("y", [g["out_rows"], DM], F32,
                            kind="ExternalOutput").ap()

    dint("attn_f", [x1_rows, DM])
    dint("wf_b", [128, 8 * DM], BF16)
    dint("pack_xbc", [CD, x1_rows], BF16)
    dint("pack_dt", [NH, x1_rows])
    dint("pack_z", [DIN, x1_rows], BF16)
    dint("x1s", [x1_rows, DM])
    dint("h2qs", [DHID, x1_rows], BF16)
    dint("s2fs", [x1_rows, 1])

    p = Prog(nc)
    PSUM = nc.alloc_psum_tensor("PSUM", [128, 4096], F32).ap()
    PSUMb = PSUM.bitcast(BF16)

    st = {"off": 18432, "marks": []}
    sb = {}

    def asb(name, shape, dtype, at=None):
        nbytes = int(np.prod(shape[1:])) * mybir.dt.size(dtype)
        off = at if at is not None else (st["off"] + 63) // 64 * 64
        h = nc.alloc_sbuf_tensor_at(name, list(shape), dtype, offset=off)
        if at is None:
            st["off"] = off + nbytes
            assert st["off"] <= 229200, (name, st["off"])
        sb[name] = h.ap()
        return sb[name]
    c2_cur = lambda: st["off"]

    def mark():
        st["marks"].append(st["off"])

    def reset():
        st["off"] = st["marks"].pop()

    asb("I128b", [128, 128], BF16)
    asb("I128f", [128, 128], F32)
    asb("ones_col", [128, 1], F32)
    asb("ones_row", [1, 128], F32)

    c = dict(nc=nc, p=p, g=g, T=T, sb=sb, asb=asb, mark=mark, reset=reset,
             PSUM=PSUM, PSUMb=PSUMb, x1_rows=x1_rows, cur_off=c2_cur)

    mark()
    alloc_A(c)
    import os
    stop = os.environ.get("K_STOP", "")
    emit_weight_prep(c)
    emit_A1(c)
    p.barrier()
    if stop != "A1":
        emit_A2(c)
        p.barrier()
        reset()
        if stop != "A2":
            emit_B(c)

    p.realize()
    return nc


def alloc_A(c):
    asb, mark, reset = c["asb"], c["mark"], c["reset"]
    cur = c["cur_off"]
    c["ov_f"] = (cur() + 63) // 64 * 64
    asb("inwT_f", [128, 4 * JP], BF16)
    c["ov_b"] = (cur() + 63) // 64 * 64
    asb("inwT_b", [128, 4 * JP], BF16)
    asb("Wf", [128, 8 * DM], BF16)
    asb("opb_bcast", [128, DM], F32)
    asb("normc", [128, 8], F32)
    for d in ("f", "b"):
        asb(f"convw_{d}", [128, 10 * 4], F32)
        asb(f"convb_{d}", [128, 10], F32)
        asb(f"dtb_{d}", [NH, 1], F32)
        asb(f"negA_{d}", [NH, 1], F32)
        asb(f"D64_{d}", [128, 8], F32)
    asb("MASKS", [128, 2 * 128], BF16)
    asb("SELB", [NH, 8 * 128], BF16)
    c["wp_off"] = (c["cur_off"]() + 63) // 64 * 64
    asb("x0_tm", [128, 2 * DM], F32)
    asb("xnn_t", [128, DM], BF16)
    asb("xnn", [128, 8 * 512], BF16)
    asb("zf", [128, 8 * 512], BF16)
    asb("xbc_pre", [128, 10 * XBS], BF16)
    asb("dtraw", [NH, 512], F32)
    asb("dtraw_b", [NH, 512], F32)
    asb("dtsp", [NH, 512], F32)
    asb("dtA16", [NH, 512], F32)
    asb("cumA", [NH, 512], F32)
    asb("negcA", [NH, 512], F32)
    asb("revA", [NH, 512], F32)
    asb("dinb16", [NH, 512], BF16)
    asb("dexpb16", [NH, 512], BF16)
    asb("S32", [128, DIN], F32)
    asb("sm", [128, 4 * 32], F32)
    asb("dt_tm", [128, NH], F32)
    asb("dexp_tm", [128, NH], F32)
    asb("dec_sb", [128, NH], BF16)
    asb("relud", [128, 1024], F32)
    asb("x_tm", [128, DIN], BF16)
    asb("Gm", [128, 128], BF16)
    asb("dg", [128, 128], F32)
    asb("sumsq", [128, 1], F32)
    for sx in ("", "2"):
        asb("cumones" + sx, [2, 16 * 128], F32)
        asb("negcumones" + sx, [2, 16 * 128], F32)
        asb("decrow" + sx, [1, NH], F32)
        asb("Dexp" + sx, [128, 16 * 128], BF16)
        asb("B_tm" + sx, [128, 128], BF16)
        asb("dtx" + sx, [128, DIN], BF16)
        asb("dtxd" + sx, [128, DIN], BF16)
        asb("Sb" + sx, [128, DIN], BF16)
        asb("dinb" + sx, [128, DIN], BF16)
        asb("yg" + sx, [128, DIN], BF16)
        asb("rstd" + sx, [128, 1], F32)
        asb("attn_st" + sx, [128, DM], F32)
    asb("accV", [128, 512], BF16)
    asb("accG", [128, 512], BF16)
    asb("ttail", [128, 30], BF16)

    asb("xbc_pre3", [128, 10 * XBS], BF16, at=c["ov_f"])
    asb("zf3", [128, 8 * 512], BF16, at=c["ov_f"] + 10336)
    ob = c["ov_b"]
    for k, (n, dt) in enumerate([("dtraw3", F32), ("dtsp3", F32),
                                 ("dtA163", F32), ("cumA3", F32),
                                 ("negcA3", F32), ("revA3", F32)]):
        asb(n, [NH, 512], dt, at=ob + k * 2048)
    asb("dinb163", [NH, 512], BF16, at=ob + 12288)
    asb("dexpb163", [NH, 512], BF16, at=ob + 13312)
    ao = (cur() + 63) // 64 * 64
    asb("attnf_in", [128, DM], F32)
    asb("rp", [128, DM], F32)
    asb("pk_rot", [128, 4 * 512], BF16, at=ao)


def emit_weight_prep(c):
    nc, p, T, sb, PSUM = c["nc"], c["p"], c["T"], c["sb"], c["PSUM"]
    asb, mark, reset = c["asb"], c["mark"], c["reset"]

    def ld(dst, src, after=()):
        return p.op("Q", lambda q: q.dma_start(
            out=dst, in_=src, allow_slow_non_contiguous=True), after=after,
            sem="QD", dma=True)

    def ldg(dst, src, after=()):
        return p.op("G", lambda g_: g_.dma_start(
            out=dst, in_=src, allow_slow_non_contiguous=True),
            after=after, sem="GD", dma=True)

    ld(sb["I128b"][:], T["I128b"][:])
    ld(sb["I128f"][:], T["I128f"][:])
    ld(sb["ones_col"][:], T["ones_col"][:])
    ld(sb["ones_row"][:], T["ones_row"][:])
    ld(sb["opb_bcast"][:], T["opb_row"].partition_broadcast(128))
    ld(sb["MASKS"][:], T["MASKS"][:])
    ld(sb["SELB"][:], T["SELB"][:])
    e_n = p.op("Q", lambda q: q.dma_start(
        out=sb["normc"][:].rearrange("p (c o) -> p c o", c=8),
        in_=T["norm_col"].rearrange("(c p) o -> p c o", p=128),
        allow_slow_non_contiguous=True), sem="QD", dma=True)
    for d in ("f", "b"):
        ld(sb[f"convw_{d}"][:].rearrange("p (c k) -> p c k", c=10),
           T[f"conv_w_{d}"].rearrange("c p k -> p c k"))
        ld(sb[f"convb_{d}"][:].rearrange("p (c o) -> p c o", c=10),
           T[f"conv_b_{d}"].rearrange("c p o -> p c o"))
        ld(sb[f"dtb_{d}"][:], T[f"dt_bias_{d}"][:])
        ld(sb[f"D64_{d}"][:], T[f"D64_{d}"][:])
        e_a = ld(sb[f"negA_{d}"][:], T[f"A_log_{d}"][:])
        e1 = p.op("S", lambda s, d=d: s.activation(
            sb[f"negA_{d}"][:], sb[f"negA_{d}"][:], Act.Exp), after=[e_a])
        p.op("V", lambda v, d=d: v.tensor_scalar_mul(
            sb[f"negA_{d}"][:], sb[f"negA_{d}"][:], -1.0), after=[e1])

    for d, lo in (("f", 0), ("b", 4)):
        w = sb[f"inwT_{d}"]
        ec = ldg(w[:].rearrange("p (c j) -> p c j", c=4),
                 T[f"in_wT_{d}"].rearrange("(c p) j -> p c j", p=128))
        for cc in range(4):
            p.op("V", lambda v, w=w, cc=cc, lo=lo: v.tensor_scalar_mul(
                w[:, cc * JP:(cc + 1) * JP], w[:, cc * JP:(cc + 1) * JP],
                sb["normc"][:, lo + cc:lo + cc + 1]), after=[ec, e_n])

    wo = c["wp_off"]
    outp = asb("wp_outp", [128, 4 * DIN], F32, at=wo)
    outpb = asb("wp_outpb", [128, 4 * DIN], BF16, at=wo + 16384)
    gn = asb("wp_gn", [128, DIN], F32, at=wo + 16384 + 8192)
    opw = asb("wp_opw", [128, 8 * DM], BF16, at=wo + 16384 + 8192 + 4096)
    e3 = ldg(opw[:].rearrange("p (c j) -> p c j", c=8),
             T["op_wT"].rearrange("(c p) j -> p c j", p=128))
    prev_rd = []
    estore = None
    Wdst = sb["Wf"]
    for d, lo in (("b", 4), ("f", 0)):
        e1 = ld(outp[:].rearrange("p (c j) -> p c j", c=4),
                T[f"outp_{d}"].rearrange("(c p) j -> p c j", p=128),
                after=prev_rd)
        e2 = ld(gn[:], T[f"gnorm_row_{d}"].partition_broadcast(128),
                after=prev_rd)
        evs = []
        for cc in range(4):
            evs.append(p.op("V", lambda v, cc=cc: v.tensor_mul(
                outpb[:, cc * DIN:(cc + 1) * DIN],
                outp[:, cc * DIN:(cc + 1) * DIN], gn[:]), after=[e1, e2]))
        prev_rd = list(evs)
        wf_evs = []
        for hp in range(8):
            for half in range(2):
                bk = (hp * 2 + half) % 4
                last = None
                for cc in range(4):
                    def mmf(t, hp=hp, half=half, cc=cc, bk=bk, lo=lo):
                        return t.matmul(
                            BK(PSUM, bk),
                            outpb[:, cc * DIN + hp * 128:cc * DIN + hp * 128 + 128],
                            opw[:, (lo + cc) * DM + half * 512:
                                (lo + cc) * DM + half * 512 + 512],
                            start=(cc == 0), stop=(cc == 3))
                    last = p.mm(mmf, [bk], after=evs + [e3], opens=(cc == 0))
                ev = p.rd("V", lambda v, hp=hp, half=half, bk=bk:
                          v.tensor_copy(Wdst[:, hp * DM + half * 512:
                                             hp * DM + half * 512 + 512],
                                        BK(PSUM, bk)), [bk],
                          after=[last] + ([estore] if estore else []))
                wf_evs.append(ev)
                prev_rd.append(ev)
        if d == "b":
            estore = p.op("Q", lambda q: q.dma_start(
                out=T["wf_b"][:], in_=Wdst[:]), after=wf_evs,
                sem="QD", dma=True)
    p.barrier()


def emit_stats(c, src, smb, after, want_fused=True, want_rp=False):
    p, sb = c["p"], c["sb"]
    sm = sb["sm"]

    def S(i):
        return sm[:, smb + i:smb + i + 1]
    sm_ = sb["sm"]
    b1 = p.op("V", lambda v: v.bn_stats(sm_[:, smb + 16:smb + 22],
                                        src[:, 0:512]), after=after)
    b2 = p.op("V", lambda v: v.bn_stats(sm_[:, smb + 22:smb + 28],
                                        src[:, 512:1024]), after=after)
    e1 = p.op("V", lambda v: v.bn_aggr(sm_[:, smb + 4:smb + 6],
                                       sm_[:, smb + 16:smb + 28]),
              after=[b1, b2])
    e_m = b1
    e_s = b2
    e5 = p.op("V", lambda v: v.tensor_scalar_add(S(6), S(5), EPS), after=[e1])
    e6 = p.op("S", lambda s: s.activation(S(7), S(6), Act.Ln), after=[e5])
    e7 = p.op("S", lambda s: s.activation(S(8), S(7), Act.Exp, scale=-0.5),
              after=[e6])
    out = dict(mean=e1, ln=e7, src_done=[e_m, e_s])
    if want_fused:
        e8 = p.op("V", lambda v: v.tensor_mul(S(9), S(8), S(8)), after=[e7])
        e8b = p.op("V", lambda v: v.tensor_mul(S(10), S(5), S(9)), after=[e8])
        e9 = p.op("V", lambda v: v.tensor_scalar_add(S(11), S(10), EPS),
                  after=[e8b])
        e10 = p.op("S", lambda s: s.activation(S(12), S(11), Act.Ln),
                   after=[e9])
        e11 = p.op("S", lambda s: s.activation(S(12), S(12), Act.Exp,
                                               scale=-0.5), after=[e10])
        out["fused"] = p.op("V", lambda v: v.tensor_mul(S(13), S(8), S(12)),
                            after=[e7, e11])
    if want_rp:
        ea = p.op("V", lambda v: v.tensor_scalar_add(S(14), S(8), 1.0),
                  after=[e7])
        eb = p.op("V", lambda v: v.tensor_mul(S(15), S(4), S(8)),
                  after=[e7, e1])
        out["rp"] = [ea, eb]
    return out


def emit_tile_front(c, c0, w, dirs, xbc_doff=4, prev_chunk_consumers=()):
    nc, p, T, sb, PSUM, PSUMb = (c["nc"], c["p"], c["T"], c["sb"], c["PSUM"],
                                 c["PSUMb"])
    res = {}
    ev_xnn = []
    prev_tp = list(prev_chunk_consumers)
    pair_prev = list(prev_chunk_consumers)
    pair_cur = []
    eq = None
    for cc in range(w):
        if cc % 2 == 0:
            w2 = min(2, w - cc)
            eq = p.op("Q", lambda q, cc=cc, w2=w2: q.dma_start(
                out=sb["x0_tm"][:, 0:w2 * DM].rearrange("p (c d) -> p c d", c=w2),
                in_=T["xc"][(c0 + cc) * CH:(c0 + cc + w2) * CH, :]
                .rearrange("(c p) d -> p c d", p=128)),
                after=pair_prev + pair_cur, sem="QD", dma=True)
            pair_prev = pair_cur
            pair_cur = []
        x0c = sb["x0_tm"][:, (cc % 2) * DM:(cc % 2) * DM + DM]
        stt = emit_stats(c, x0c, (cc % 4) * 32, [eq], want_fused=True,
                         want_rp=True)
        sm = sb["sm"]
        smb = (cc % 4) * 32
        e13 = p.op("V", lambda v, x0c=x0c, smb=smb: v.tensor_scalar(
            sb["xnn_t"][:], x0c, sm[:, smb + 4:smb + 5], sm[:, smb + 13:smb + 14],
            Alu.subtract, Alu.mult), after=[stt["fused"], stt["mean"]] + prev_tp)
        pair_cur += [e13] + stt["src_done"]
        res.setdefault("rp", {})[cc] = stt.get("rp", [])
        prev_tp = []
        for b8 in range(8):
            bk = 6 + b8 % 2
            et = p.mm(lambda t, b8=b8, bk=bk: t.transpose(
                PSUMb[:, bk * 1024:bk * 1024 + 128],
                sb["xnn_t"][:, b8 * 128:b8 * 128 + 128], sb["I128b"][:]),
                [bk], after=[e13], opens=True)
            prev_tp.append(et)
            ev_xnn.append(p.rd("S", lambda s, b8=b8, bk=bk, cc=cc:
                               s.activation(
                sb["xnn"][:, b8 * 512 + cc * 128:b8 * 512 + cc * 128 + 128],
                PSUMb[:, bk * 1024:bk * 1024 + 128], Act.Copy),
                [bk], after=[et]))
    res["ln"] = list(ev_xnn)
    res["x0"] = eq

    for (d, mode, zdst, xdst, xoff) in dirs:
        wmat = sb[f"inwT_{d}"]
        blo = 0 if d == "f" else 4
        jlist = ([NJ - 1] + list(range(NJ - 1)) if mode == "full"
                 else [NJ - 1] + list(range(8, NJ - 1)))
        ev_ev = []
        ev_dt = None
        for j in jlist:
            bk = j % 4
            last = None
            for ccn in range(4):
                def mmf(t, j=j, ccn=ccn, bk=bk, w=w, blo=blo, wmat=wmat):
                    return t.matmul(
                        BK(PSUM, bk, 0, w * 128),
                        wmat[:, ccn * JP + j * 128:ccn * JP + j * 128 + 128],
                        sb["xnn"][:, (blo + ccn) * 512:(blo + ccn) * 512 + w * 128],
                        start=(ccn == 0), stop=(ccn == 3))
                last = p.mm(mmf, [bk], after=ev_xnn, opens=(ccn == 0))
            pack_dir = zdst is not None and zdst is not sb["zf"]
            if pack_dir and j < 18:
                pst = c["pk_state"]
                slot = pst["idx"] % 4
                pst["idx"] += 1
                war = [e for e in [pst["dma"][slot]] if e is not None]
                if j < 8:
                    rows = ("pack_z", j * 128)
                    fn = Act.Silu
                else:
                    rows = ("pack_xbc", (j - 8) * 128)
                    fn = Act.Copy
                ev = p.rd("S", lambda s, bk=bk, w=w, slot=slot, fn=fn:
                          s.activation(
                              sb["pk_rot"][:, slot * 512:slot * 512 + w * 128],
                              BK(PSUM, bk, 0, w * 128), fn),
                          [bk], after=[last] + war)
                ed = p.op("Q", lambda q, rows=rows, slot=slot, w=w,
                          x1o=pst["x1o"]:
                          q.dma_start(
                              out=T[rows[0]][rows[1]:rows[1] + 128,
                                             x1o:x1o + w * 128],
                              in_=sb["pk_rot"][:, slot * 512:slot * 512 + w * 128]),
                          after=[ev], sem="QD", dma=True)
                pst["dma"][slot] = ed
                ev_ev.append(ed)
                continue
            if j < 8:
                if pack_dir:
                    ev_ev.append(p.rd("S", lambda s, j=j, bk=bk, w=w, zdst=zdst:
                                      s.activation(
                                          zdst[:, j * 512:j * 512 + w * 128],
                                          BK(PSUM, bk, 0, w * 128), Act.Copy),
                                      [bk], after=[last]))
                else:
                    ev_ev.append(p.rd("V", lambda v, j=j, bk=bk, w=w, zdst=zdst:
                                      v.tensor_copy(
                                          zdst[:, j * 512:j * 512 + w * 128],
                                          BK(PSUM, bk, 0, w * 128)),
                                      [bk], after=[last]))
            elif j < 18:
                stride = XBS if xdst is sb["xbc_pre"] else 512
                if pack_dir:
                    ev_ev.append(p.rd("S", lambda s, j=j, bk=bk, w=w, xdst=xdst,
                                      xoff=xoff, stride=stride: s.activation(
                                          xdst[:, (j - 8) * stride + xoff:
                                               (j - 8) * stride + xoff + w * 128],
                                          BK(PSUM, bk, 0, w * 128), Act.Copy),
                                      [bk], after=[last]))
                else:
                    ev_ev.append(p.rd("V", lambda v, j=j, bk=bk, w=w, xdst=xdst,
                                      xoff=xoff, stride=stride: v.tensor_copy(
                                          xdst[:, (j - 8) * stride + xoff:
                                               (j - 8) * stride + xoff + w * 128],
                                          BK(PSUM, bk, 0, w * 128)),
                                      [bk], after=[last]))
            else:
                dtdst = sb["dtraw_b"] if (d == "b" and mode == "full") else sb["dtraw"]
                ev_dt = p.rd("V", lambda v, bk=bk, w=w, dtdst=dtdst: v.tensor_copy(
                    dtdst[:, 0:w * 128],
                    PSUM[0:16, bk * 512:bk * 512 + w * 128]), [bk], after=[last])
                ev_ev.append(ev_dt)
        res[d] = dict(evac=ev_ev, dt=ev_dt)
    return res


def emit_conv_silu(c, d, w, src_evs):
    p, sb = c["p"], c["sb"]
    ts = c.get("tsfx", "")
    nts = c.get("tsfx_next", ts)

    def TB(n, s=None):
        return sb[n + (ts if s is None else s)]
    fwd = (d == "f")
    cw, cb = sb[f"convw_{d}"], sb[f"convb_{d}"]
    doff = 4 if fwd else (512 - w * 128)
    offs = [1, 2, 3, 4] if fwd else [doff, doff + 1, doff + 2, doff + 3]
    wof = [0, 1, 2, 3] if fwd else [3, 2, 1, 0]
    src_lo = (4 + w * 128 - 3) if fwd else doff
    e_st = p.op("V", lambda v: v.tensor_copy(
        sb["ttail"][:].rearrange("p (b s) -> p b s", b=10),
        TB("xbc_pre").rearrange("p (b s) -> p b s", b=10)
        [:, :, src_lo:src_lo + 3]), after=list(src_evs))
    evs = []
    tap_evs = []
    ACCN = ["accV", "accG"]
    last_sil = {i: None for i in range(2)}
    for blk in range(10):
        acc = sb[ACCN[blk % 2]]
        pe = last_sil[blk % 2]
        e = p.op("V", lambda v, blk=blk, w=w, acc=acc: v.tensor_scalar(
            acc[:, 0:w * 128],
            TB("xbc_pre")[:, blk * XBS + offs[0]:blk * XBS + offs[0] + w * 128],
            cw[:, blk * 4 + wof[0]:blk * 4 + wof[0] + 1],
            cb[:, blk:blk + 1], Alu.mult, Alu.add),
            after=list(src_evs) + [e_st] + ([pe] if pe else []))
        for k in range(1, 4):
            e = p.op("V", lambda v, blk=blk, k=k, w=w, acc=acc:
                     v.scalar_tensor_tensor(
                acc[:, 0:w * 128],
                TB("xbc_pre")[:, blk * XBS + offs[k]:blk * XBS + offs[k] + w * 128],
                cw[:, blk * 4 + wof[k]:blk * 4 + wof[k] + 1],
                acc[:, 0:w * 128], Alu.mult, Alu.add), after=[e])
        tap_evs.append(e)
        se = p.op("S", lambda s, blk=blk, w=w, acc=acc: s.activation(
            TB("xbc_pre")[:, blk * XBS + doff:blk * XBS + doff + w * 128],
            acc[:, 0:w * 128], Act.Silu), after=[e])
        last_sil[blk % 2] = se
        evs.append(se)
    slot_lo = 1 if fwd else 512
    tail_ev = p.op("V", lambda v: v.tensor_copy(
        TB("xbc_pre", nts).rearrange("p (b s) -> p b s", b=10)
        [:, :, slot_lo:slot_lo + 3],
        sb["ttail"][:].rearrange("p (b s) -> p b s", b=10)),
        after=tap_evs + [e_st])
    return evs, tail_ev, doff


def emit_dt_pipeline(c, d, w, ev_dt):
    p, sb = c["p"], c["sb"]
    ts = c.get("tsfx", "")

    def TB(n):
        return sb[n + ts]
    e1 = p.op("S", lambda s: s.activation(TB("dtsp")[:, 0:w * 128],
                                          TB("dtraw")[:, 0:w * 128], Act.Exp,
                                          bias=sb[f"dtb_{d}"][:]), after=[ev_dt])
    e2 = p.op("S", lambda s: s.activation(TB("dtsp")[:, 0:w * 128],
                                          TB("dtsp")[:, 0:w * 128], Act.Ln,
                                          bias=1.0), after=[e1])
    e3 = p.op("V", lambda v: v.tensor_scalar_mul(TB("dtA16")[:, 0:w * 128],
                                                 TB("dtsp")[:, 0:w * 128],
                                                 sb[f"negA_{d}"][:]), after=[e2])
    e4 = e3
    for cl in range(w):
        e4 = p.op("V", lambda v, cl=cl: v.tensor_tensor_scan(
            TB("cumA")[:, cl * 128:(cl + 1) * 128],
            TB("dtA16")[:, cl * 128:(cl + 1) * 128],
            TB("dtA16")[:, cl * 128:(cl + 1) * 128], 0.0,
            Alu.add, Alu.bypass), after=[e4])
    e6 = p.op("V", lambda v: v.scalar_tensor_tensor(
        TB("revA")[:, 0:w * 128].rearrange("p (c t) -> p c t", c=w),
        TB("cumA")[:, 0:w * 128].rearrange("p (c t) -> p c t", c=w), -1.0,
        TB("cumA")[:, 0:w * 128].rearrange("p (c t) -> p c t", c=w)
        [:, :, 127:128].broadcast_to([NH, w, 128]),
        Alu.mult, Alu.add), after=[e4])
    if d == "b":
        c1 = p.op("V", lambda v: v.tensor_sub(TB("negcA")[:, 0:w * 128],
                                              TB("cumA")[:, 0:w * 128],
                                              TB("dtA16")[:, 0:w * 128]),
                  after=[e4])
        c2 = p.op("V", lambda v: v.tensor_add(TB("cumA")[:, 0:w * 128],
                                              TB("revA")[:, 0:w * 128],
                                              TB("dtA16")[:, 0:w * 128]),
                  after=[e6, c1])
        c3 = p.op("V", lambda v: v.tensor_copy(TB("revA")[:, 0:w * 128],
                                               TB("negcA")[:, 0:w * 128]),
                  after=[c1, c2])
        e4 = c2
        e6 = c3
    e5 = p.op("V", lambda v: v.tensor_scalar_mul(TB("negcA")[:, 0:w * 128],
                                                 TB("cumA")[:, 0:w * 128],
                                                 -1.0), after=[e4, e6])
    e7 = p.op("S", lambda s: s.activation(TB("dinb16")[:, 0:w * 128],
                                          TB("cumA")[:, 0:w * 128], Act.Exp),
              after=[e4])
    e8 = p.op("S", lambda s: s.activation(TB("dexpb16")[:, 0:w * 128],
                                          TB("revA")[:, 0:w * 128], Act.Exp),
              after=[e6])
    return e2, dict(dtA=e3, cum=e4, neg=e5, rev=e6, din=e7, dexp=e8)


def emit_zsilu(c, w, src_ev):
    p, sb = c["p"], c["sb"]
    ts = c.get("tsfx", "")

    def TB(n):
        return sb[n + ts]
    evs = []
    for b8 in range(8):
        evs.append(p.op("S", lambda s, b8=b8, w=w: s.activation(
            TB("zf")[:, b8 * 512:b8 * 512 + w * 128],
            TB("zf")[:, b8 * 512:b8 * 512 + w * 128], Act.Silu),
            after=list(src_ev)))
    return evs


def ssd_front(c, d, o, full, deps, par, xbase=4):
    nc, p, sb, PSUM, PSUMb = c["nc"], c["p"], c["sb"], c["PSUM"], c["PSUMb"]
    fwd = (d == "f")
    MSK = sb["MASKS"][:, 0:128] if fwd else sb["MASKS"][:, 128:256]
    I128f, I128b = sb["I128f"], sb["I128b"]
    conv_evs = deps["conv"]
    B = 4 * par
    sfx = "2" if par else ""
    ts = c.get("tsfx", "")

    def P(n):
        return sb[n + sfx]

    def TB(n):
        return sb[n + ts]

    e1 = p.mm(lambda t: t.transpose(BK(PSUM, B + 0, 0, 16),
                                    TB("dtsp")[:, o:o + 128],
                                    I128f[0:16, 0:16]),
              [B + 0], after=[deps["dt"]], opens=True)
    ev_dtm = p.rd("V", lambda v: v.tensor_copy(sb["dt_tm"][:],
                                               BK(PSUM, B + 0, 0, 16)),
                  [B + 0], after=[e1])
    e2 = p.mm(lambda t: t.transpose(
        PSUMb[:, (B + 1) * 1024:(B + 1) * 1024 + 16],
        TB("dexpb16")[:, o:o + 128], I128b[0:16, 0:16]),
        [B + 1], after=[deps["dexp"]], opens=True)
    ev_dexptm = p.rd("V", lambda v: v.tensor_copy(
        sb["dexp_tm"][:], PSUMb[:, (B + 1) * 1024:(B + 1) * 1024 + 16]),
        [B + 1], after=[e2])
    dcol = o + 127 if fwd else o
    e3 = p.mm(lambda t: t.transpose(
        PSUM[0:1, (B + 1) * 512 + 32:(B + 1) * 512 + 48],
        TB("cumA")[:, dcol:dcol + 1], I128f[0:16, 0:16]), [B + 1],
        after=[deps["cum"]], opens=True)
    e3b = p.rd("S", lambda s: s.activation(
        P("decrow")[:], PSUM[0:1, (B + 1) * 512 + 32:(B + 1) * 512 + 48],
        Act.Exp), [B + 1], after=[e3])
    e4 = p.mm(lambda t: t.matmul(BK(PSUM, B + 0, 16, 32), sb["ones_row"][:],
                                 P("decrow")[:], start=True, stop=True,
                                 skip_group_check=True),
              [B + 0], after=[e3b], opens=True)
    ev_dec = p.rd("V", lambda v: v.tensor_copy(sb["dec_sb"][:],
                                               BK(PSUM, B + 0, 16, 32)),
                  [B + 0], after=[e4])
    ev_xt = []
    for b8 in range(8):
        bk = B + 2 + (b8 % 2)
        et = p.mm(lambda t, b8=b8, bk=bk: t.transpose(
            PSUMb[:, bk * 1024:bk * 1024 + 128],
            TB("xbc_pre")[:, b8 * XBS + xbase + o:b8 * XBS + xbase + o + 128],
            I128b[:]), [bk], after=[conv_evs[b8]], opens=True)
        ev_xt.append(p.rd("V", lambda v, b8=b8, bk=bk: v.tensor_copy(
            sb["x_tm"][:, b8 * 128:b8 * 128 + 128],
            PSUMb[:, bk * 1024:bk * 1024 + 128]), [bk], after=[et]))
    et = p.mm(lambda t: t.transpose(
        PSUMb[:, (B + 2) * 1024:(B + 2) * 1024 + 128],
        TB("xbc_pre")[:, 8 * XBS + xbase + o:8 * XBS + xbase + o + 128],
        I128b[:]), [B + 2], after=[conv_evs[8]], opens=True)
    ev_bt = p.rd("V", lambda v: v.tensor_copy(
        P("B_tm")[:], PSUMb[:, (B + 2) * 1024:(B + 2) * 1024 + 128]),
        [B + 2], after=[et])
    ev_dtx = p.op("V", lambda v: v.tensor_tensor(
        P("dtx").rearrange("p (h q) -> p h q", h=NH),
        sb["x_tm"].rearrange("p (h q) -> p h q", h=NH),
        sb["dt_tm"][:].unsqueeze(2).broadcast_to([128, NH, HD]), Alu.mult),
        after=ev_xt + [ev_dtm])
    ev_dtxd = p.op("V", lambda v: v.tensor_tensor(
        P("dtxd").rearrange("p (h q) -> p h q", h=NH),
        P("dtx").rearrange("p (h q) -> p h q", h=NH),
        sb["dexp_tm"][:].unsqueeze(2).broadcast_to([128, NH, HD]),
        Alu.mult), after=[ev_dtx, ev_dexptm])
    out = dict(ev_dtxd=ev_dtxd, ev_bt=ev_bt, ev_dec=ev_dec)
    if full:
        stw = [deps["stk_war"]] if deps.get("stk_war") else []
        esA = p.op("G", lambda g_: g_.dma_start(
            out=P("cumones")[1:2, :].rearrange("o (h s) -> o h s", h=16),
            in_=TB("cumA")[:, o:o + 128]), after=[deps["cum"]] + stw,
            sem="GD", dma=True)
        esB = p.op("G", lambda g_: g_.dma_start(
            out=P("negcumones")[0:1, :].rearrange("o (h s) -> o h s", h=16),
            in_=TB("negcA")[:, o:o + 128]), after=[deps["neg"]] + stw,
            sem="GD", dma=True)
        ev_ex = []
        for half in range(2):
            mm_d = []
            for bq in range(2):
                bk = B + bq
                for hq in range(4):
                    h = half * 8 + bq * 4 + hq
                    def mma(t, h=h, bk=bk, hq=hq):
                        return t.matmul(
                            BK(PSUM, bk, hq * 128, hq * 128 + 128),
                            P("cumones")[0:2, h * 128:(h + 1) * 128],
                            P("negcumones")[0:2, h * 128:(h + 1) * 128],
                            start=(hq == 0), stop=(hq == 3))
                    mm_d.append(p.mm(mma, [bk], after=[esA, esB],
                                     opens=(hq == 0)))
            rl = [c["relud_last"]] if c.get("relud_last") else []
            er = p.rd("S", lambda s: s.activation(
                sb["relud"][:], PSUM[:, B * 512:B * 512 + 1024],
                Act.Relu), [B + 0, B + 1], after=mm_d + rl)
            ev_ex.append(p.op("S", lambda s, half=half: s.activation(
                P("Dexp")[:, half * 1024:half * 1024 + 1024], sb["relud"][:],
                Act.Exp, scale=-1.0), after=[er]))
            c["relud_last"] = ev_ex[-1]
        out["stk"] = ev_ex[-1]
        e7 = p.mm(lambda t: t.matmul(
            BK(PSUM, B + 2, 0, 128),
            TB("xbc_pre")[:, 8 * XBS + xbase + o:8 * XBS + xbase + o + 128],
            TB("xbc_pre")[:, 9 * XBS + xbase + o:9 * XBS + xbase + o + 128],
            start=True, stop=True), [B + 2],
            after=[conv_evs[8], conv_evs[9]], opens=True)
        ev_gm = p.rd("V", lambda v: v.tensor_tensor(
            sb["Gm"][:], BK(PSUM, B + 2, 0, 128), MSK, Alu.mult),
            [B + 2], after=[e7])
        out["ev_m"] = p.op("V", lambda v: v.tensor_tensor(
            P("Dexp").rearrange("p (h q) -> p h q", h=NH),
            P("Dexp").rearrange("p (h q) -> p h q", h=NH),
            sb["Gm"][:].unsqueeze(1).broadcast_to([128, NH, 128]), Alu.mult),
            after=ev_ex + [ev_gm])
    mm_s = []
    for half in range(2):
        bk = B + 2 + half
        def mmf(t, half=half, bk=bk):
            return t.matmul(BK(PSUM, bk, 0, 512), P("B_tm")[:],
                            P("dtxd")[:, half * 512:half * 512 + 512],
                            start=True, stop=True)
        mm_s.append(p.mm(mmf, [bk], after=[ev_dtxd, ev_bt], opens=True))
    out["mm_s"] = mm_s
    return out


def ssd_back(c, d, o, full, fr, deps, attn_fn, par, xbase=4):
    nc, p, sb, PSUM, PSUMb = c["nc"], c["p"], c["sb"], c["PSUM"], c["PSUMb"]
    B = 4 * par
    sfx = "2" if par else ""
    osfx = "" if par else "2"
    ts = c.get("tsfx", "")

    def P(n):
        return sb[n + sfx]

    def TB(n):
        return sb[n + ts]
    Sb_prev = sb["Sb" + osfx]
    conv_evs = deps["conv"]
    mm_s = fr["mm_s"]
    emit_attn = attn_fn() if (full and attn_fn is not None) else None
    sbw = [deps["sb_war"]] if deps.get("sb_war") else []
    e1 = p.op("V", lambda v: v.tensor_tensor(
        P("Sb").rearrange("p (h q) -> p h q", h=NH),
        sb["S32"].rearrange("p (h q) -> p h q", h=NH),
        sb["dec_sb"][:].unsqueeze(2).broadcast_to([128, NH, HD]), Alu.mult),
        after=[fr["ev_dec"], deps["sb_ready"]] + sbw)
    e2 = p.rd("V", lambda v: v.tensor_add(sb["S32"][:, 0:512],
                                          P("Sb")[:, 0:512],
                                          BK(PSUM, B + 2, 0, 512)), [B + 2],
              after=[e1] + mm_s)
    e3 = p.rd("V", lambda v: v.tensor_add(sb["S32"][:, 512:1024],
                                          P("Sb")[:, 512:1024],
                                          BK(PSUM, B + 3, 0, 512)), [B + 3],
              after=[e1] + mm_s)
    e4 = p.op("V", lambda v: v.tensor_copy(P("Sb")[:], sb["S32"][:]),
              after=[e2, e3])
    res = dict(ev_state=e4, stk=fr.get("stk"), mm_z=[], attn=[])
    if not full:
        return res
    mm_din = []
    for ci in range(8):
        bk = B + ci // 4
        pos = (ci % 4) * 128
        def mmf(t, ci=ci, bk=bk, pos=pos):
            return t.matmul(BK(PSUM, bk, pos, pos + 128),
                            sb["SELB"][:, ci * 128:ci * 128 + 128],
                            TB("dinb16")[:, o:o + 128], start=(ci % 4 == 0),
                            stop=(ci % 4 == 3))
        mm_din.append(p.mm(mmf, [bk], after=[deps["din"]],
                           opens=(ci % 4 == 0)))
    ev_di1 = p.rd("V", lambda v: v.tensor_copy(P("dinb")[:, 0:512],
                                               BK(PSUM, B + 0, 0, 512)),
                  [B + 0], after=mm_din)
    ev_di2 = p.rd("V", lambda v: v.tensor_copy(P("dinb")[:, 512:1024],
                                               BK(PSUM, B + 1, 0, 512)),
                  [B + 1], after=mm_din)
    mm_z = []
    for ci in range(8):
        bk = B + 2 + ci // 4
        pos = (ci % 4) * 128
        def mmf(t, ci=ci, bk=bk, pos=pos):
            return t.matmul(
                BK(PSUM, bk, pos, pos + 128),
                Sb_prev[:, ci * 128:ci * 128 + 128],
                TB("xbc_pre")[:, 9 * XBS + xbase + o:9 * XBS + xbase + o + 128],
                start=(ci % 4 == 0), stop=(ci % 4 == 3))
        mm_z.append(p.mm(mmf, [bk],
                         after=[conv_evs[9], deps["sb_ready"]],
                         opens=(ci % 4 == 0)))
    res["mm_z"] = mm_z
    mm_y = []
    for h in range(NH):
        ci = h // 2
        bk = B + ci // 4
        pos = (ci % 4) * 128
        pb = 64 * (h % 2)
        def mmf(t, h=h, bk=bk, pos=pos, pb=pb):
            return t.matmul(
                PSUM[pb:pb + 64, bk * 512 + pos:bk * 512 + pos + 128],
                P("dtx")[:, h * 64:h * 64 + 64],
                P("Dexp")[:, h * 128:h * 128 + 128],
                start=True, stop=True, skip_group_check=True,
                tile_position=(0, pb))
        mm_y.append(p.mm(mmf, [bk], after=[fr["ev_m"], fr["ev_dtxd"]],
                         opens=(h in (0, 8))))
    d64 = sb[f"D64_{d}"]
    ev_g1a = p.rd("V", lambda v: v.tensor_tensor(
        P("dinb")[:, 0:512], P("dinb")[:, 0:512], BK(PSUM, B + 2, 0, 512),
        Alu.mult), [B + 2], after=mm_z + [ev_di1])
    ev_g1b = p.rd("V", lambda v: v.tensor_tensor(
        P("dinb")[:, 512:1024], P("dinb")[:, 512:1024],
        BK(PSUM, B + 3, 0, 512), Alu.mult), [B + 3], after=mm_z + [ev_di2])
    ev_g2a = p.rd("V", lambda v: v.tensor_add(
        P("dinb")[:, 0:512], P("dinb")[:, 0:512], BK(PSUM, B + 0, 0, 512)),
        [B + 0], after=mm_y + [ev_g1a])
    ev_g2b = p.rd("V", lambda v: v.tensor_add(
        P("dinb")[:, 512:1024], P("dinb")[:, 512:1024],
        BK(PSUM, B + 1, 0, 512)), [B + 1], after=mm_y + [ev_g1b])
    ev_dterm = p.op("V", lambda v: v.tensor_tensor(
        P("dtx").rearrange("p (a q) -> p a q", a=8),
        TB("xbc_pre").rearrange("p (a q) -> p a q", a=10)
        [:, 0:8, xbase + o:xbase + o + 128],
        d64[:].unsqueeze(2).broadcast_to([128, 8, 128]), Alu.mult),
        after=[ev_g2a, ev_g2b])
    ev_g3 = p.op("V", lambda v: v.tensor_add(P("dinb")[:], P("dinb")[:],
                                             P("dtx")[:]),
                 after=[ev_g2a, ev_g2b, ev_dterm])
    ev_yg = p.op("V", lambda v: v.tensor_tensor(
        P("yg").rearrange("p (a q) -> p a q", a=8),
        P("dinb").rearrange("p (a q) -> p a q", a=8),
        TB("zf").rearrange("p (a q) -> p a q", a=8)[:, :, o:o + 128],
        Alu.mult), after=[ev_g3, deps["z"]])
    mm_g = []
    for ci in range(8):
        def mmf(t, ci=ci):
            return t.matmul(BK(PSUM, B + 2, 0, 128),
                            P("yg")[:, ci * 128:ci * 128 + 128],
                            P("yg")[:, ci * 128:ci * 128 + 128],
                            start=(ci == 0), stop=(ci == 7))
        mm_g.append(p.mm(mmf, [B + 2], after=[ev_yg], opens=(ci == 0)))
    ev_dg = p.rd("V", lambda v: v.tensor_tensor(
        sb["dg"][:], BK(PSUM, B + 2, 0, 128), sb["I128f"][:], Alu.mult),
        [B + 2], after=mm_g)
    ev_ss = p.op("V", lambda v: v.reduce_sum(sb["sumsq"][:], sb["dg"][:],
                                             axis=AX.X), after=[ev_dg])
    ev_r1 = p.op("V", lambda v: v.tensor_scalar(P("rstd")[:], sb["sumsq"][:],
                                                1.0 / DIN, EPS, Alu.mult,
                                                Alu.add), after=[ev_ss])
    ev_r2 = p.op("S", lambda s: s.activation(P("rstd")[:], P("rstd")[:],
                                             Act.Ln), after=[ev_r1])
    ev_r3 = p.op("S", lambda s: s.activation(P("rstd")[:], P("rstd")[:],
                                             Act.Exp, scale=-0.5),
                 after=[ev_r2])
    Wf = sb["Wf"]
    wf_dep = c.get("wf_ev") or []
    mm_o = []
    for half in range(2):
        bk = B + half
        for ci in range(8):
            def mmf(t, half=half, ci=ci, bk=bk):
                return t.matmul(
                    BK(PSUM, bk, 0, 512),
                    P("yg")[:, ci * 128:ci * 128 + 128],
                    Wf[:, ci * DM + half * 512:ci * DM + half * 512 + 512],
                    start=(ci == 0), stop=(ci == 7))
            mm_o.append(p.mm(mmf, [bk], after=[ev_yg] + wf_dep,
                             opens=(ci == 0)))
    res["attn"] = emit_attn(mm_o, ev_r3, B, P("rstd"))
    return res


def run_pairs(c, st, d, order, evd, ev_sp, ev_conv, ev_z, full_fn, attn_mk,
              xbase=4):
    p = c["p"]
    tails = []
    for i in range(0, len(order), 2):
        grp = order[i:i + 2]
        frs = []
        for cl in grp:
            o = cl * 128
            par = st["seq"] % 2
            full = full_fn(cl)
            deps = dict(dt=ev_sp, cum=evd["cum"], neg=evd["neg"],
                        din=evd["din"], dexp=evd["dexp"], conv=ev_conv,
                        stk_war=st["stk"][par],
                        z=(ev_z[-1] if ev_z else None))
            fr = ssd_front(c, d, o, full, deps, par, xbase)
            frs.append((cl, o, par, full, fr, deps))
            st["seq"] += 1
        for (cl, o, par, full, fr, deps) in frs:
            deps["sb_ready"] = st["sb_ready"]
            deps["sb_war"] = st["sb_war"]
            bk = ssd_back(c, d, o, full, fr, deps,
                          (attn_mk(cl) if attn_mk else None), par, xbase)
            st["sb_ready"] = bk["ev_state"]
            if bk["mm_z"]:
                st["sb_war"] = bk["mm_z"][-1]
            if bk.get("stk"):
                st["stk"][par] = bk["stk"]
            tails.append(bk)
    return tails


def emit_A1(c):
    nc, p, g, T, sb, PSUM = c["nc"], c["p"], c["g"], c["T"], c["sb"], c["PSUM"]
    win, f_hi = g["win"], g["f_hi"]
    c["tsfx"] = ""
    c["tsfx_next"] = ""
    c["pk_state"] = dict(idx=0, dma=[None] * 4, x1o=0)
    for n in ("cumones", "negcumones", "cumones2", "negcumones2"):
        p.op("V", lambda v, n=n: v.memset(sb[n][0:2, :], 1.0))
    p.op("V", lambda v: v.memset(sb["S32"][:], 0.0))
    e0d = p.op("G", lambda g_: g_.memset(sb["Sb"][:], 0.0))
    e0d2 = p.op("G", lambda g_: g_.memset(sb["Sb2"][:], 0.0))
    e0e = p.op("V", lambda v: v.memset(sb["xbc_pre"][:], 0.0))
    st = dict(seq=0, sb_ready=e0d, sb_war=e0d2, stk={0: None, 1: None})
    aprev = {0: None, 1: None}
    prev_pack = []
    prev_tail = e0e
    prev_consumers = []
    for (c0, c1) in _tiles(0, f_hi):
        w = c1 - c0
        full_tile = c1 > win
        dirs = [("f", "full" if full_tile else "xbcdt", sb["zf"],
                 sb["xbc_pre"], 4)]
        packing = c0 >= win
        if packing:
            c["pk_state"]["x1o"] = (c0 - win) * CH
            dirs.append(("b", "full", sb["pk_rot"], sb["pk_rot"], 0))
        fr = emit_tile_front(c, c0, w, dirs,
                             prev_chunk_consumers=prev_consumers)
        ev_conv, prev_tail_new, _doff = emit_conv_silu(
            c, "f", w, fr["f"]["evac"] + [prev_tail])
        ev_sp, evd = emit_dt_pipeline(c, "f", w, fr["f"]["dt"])
        ev_z = emit_zsilu(c, w, fr["f"]["evac"]) if full_tile else []
        if packing:
            x1o = (c0 - win) * CH
            pk = [p.op("Q", lambda q, x1o=x1o, w=w: q.dma_start(
                out=T["pack_dt"][:, x1o:x1o + w * 128],
                in_=sb["dtraw_b"][:, 0:w * 128]),
                after=fr["b"]["evac"] + prev_pack, sem="QD", dma=True)]
            prev_pack = pk

        def attn_mk(cl):
            lc = c0 + cl
            if lc < win:
                return None
            x1o = (lc - win) * CH
            rp_evs = fr["rp"][cl]

            def factory():
                smb = (cl % 4) * 32

                def emit_attn(mm_o, ev_r, B, rstd):
                    par = B // 4
                    ast_b = sb["attn_st" + ("2" if par else "")]
                    war = [aprev[par]] if aprev[par] else []
                    eqx = p.op("Q", lambda q: q.dma_start(
                        out=ast_b[:], in_=T["xc"][lc * CH:(lc + 1) * CH, :]),
                        after=war, sem="QD", dma=True)
                    er1 = p.op("V", lambda v: v.tensor_scalar(
                        ast_b[:], ast_b[:],
                        sb["sm"][:, smb + 14:smb + 15],
                        sb["sm"][:, smb + 15:smb + 16],
                        Alu.mult, Alu.subtract), after=rp_evs + [eqx])
                    er2 = p.op("V", lambda v: v.tensor_add(
                        ast_b[:], ast_b[:], sb["opb_bcast"][:]),
                        after=[er1])
                    e1 = p.rd("V", lambda v: v.scalar_tensor_tensor(
                        ast_b[:, 0:512], BK(PSUM, B + 0, 0, 512),
                        rstd, ast_b[:, 0:512],
                        Alu.mult, Alu.add), [B + 0],
                        after=mm_o + [ev_r, er2])
                    e2 = p.rd("V", lambda v: v.scalar_tensor_tensor(
                        ast_b[:, 512:1024], BK(PSUM, B + 1, 0, 512),
                        rstd, ast_b[:, 512:1024],
                        Alu.mult, Alu.add), [B + 1],
                        after=mm_o + [ev_r, er2])
                    ed = p.op("Q", lambda q: q.dma_start(
                        out=T["attn_f"][x1o:x1o + CH, :], in_=ast_b[:]),
                        after=[e1, e2], sem="QD", dma=True)
                    aprev[par] = ed
                    return [ed]
                return emit_attn
            return factory

        tails = run_pairs(c, st, "f", list(range(w)), evd, ev_sp, ev_conv,
                          ev_z, lambda cl: c0 + cl >= win, attn_mk, xbase=4)
        prev_tail = prev_tail_new
        chunk_tail = []
        for bk in tails:
            chunk_tail.append(bk["ev_state"])
            chunk_tail += bk["attn"]
        prev_consumers = chunk_tail + [prev_tail] + prev_pack


def emit_A2(c):
    nc, p, g, T, sb, PSUM = c["nc"], c["p"], c["g"], c["T"], c["sb"], c["PSUM"]
    win, span = g["win"], g["span"]
    ev_wf = p.op("Q", lambda q: q.dma_start(out=sb["Wf"][:],
                                            in_=T["wf_b"][:]),
                 sem="QD", dma=True)
    c["wf_ev"] = [ev_wf]
    p.op("V", lambda v: v.memset(sb["S32"][:], 0.0))
    e0d = p.op("G", lambda g_: g_.memset(sb["Sb"][:], 0.0))
    e0d2 = p.op("G", lambda g_: g_.memset(sb["Sb2"][:], 0.0))
    e0e = p.op("V", lambda v: v.memset(sb["xbc_pre"][:], 0.0))
    e0e3 = p.op("V", lambda v: v.memset(sb["xbc_pre3"][:], 0.0))
    st = dict(seq=0, sb_ready=e0d, sb_war=e0d2, stk={0: None, 1: None})
    ast = dict(prev_v=[], prev_ed=None, x0_done=[],
               prev_ed_par={0: None, 1: None})
    prev_tail = e0e
    prev_cons_t = {"": [], "3": [e0e3]}
    prev_consumers = []
    tiles = [(span - win, span)] + list(reversed(_tiles(win, span - win)))
    first = True
    for ti, (c0, c1) in enumerate(tiles):
        ts = "" if ti % 2 == 0 else "3"
        c["tsfx"] = ts
        c["tsfx_next"] = "3" if ts == "" else ""
        w = c1 - c0
        is_win = first
        first = False
        doff = 512 - w * 128
        if is_win:
            fr = emit_tile_front(c, c0, w, [("b", "xbcdt", None,
                                             sb["xbc_pre"], doff)],
                                 prev_chunk_consumers=prev_consumers)
            src_evs = fr["b"]["evac"] + [prev_tail]
            ev_dtraw = fr["b"]["dt"]
            ev_z = []
        else:
            x1o = (c0 - win) * CH
            pc = list(prev_cons_t[ts])
            e1 = p.op("S", lambda q, x1o=x1o, w=w, doff=doff, ts=ts:
                      q.dma_start(
                out=sb["xbc_pre" + ts].rearrange("p (b s) -> p b s", b=10)
                [:, :, doff:doff + w * 128],
                in_=T["pack_xbc"][:, x1o:x1o + w * 128]
                .rearrange("(c p) t -> p c t", p=128)),
                after=pc, sem="SD", dma=True)
            e2 = p.op("S", lambda q, x1o=x1o, w=w, ts=ts: q.dma_start(
                out=sb["zf" + ts].rearrange("p (b s) -> p b s", b=8)
                [:, :, 0:w * 128],
                in_=T["pack_z"][:, x1o:x1o + w * 128]
                .rearrange("(c p) t -> p c t", p=128)),
                after=pc, sem="SD", dma=True)
            e3 = p.op("S", lambda q, x1o=x1o, w=w, ts=ts: q.dma_start(
                out=sb["dtraw" + ts][:, 0:w * 128],
                in_=T["pack_dt"][:, x1o:x1o + w * 128]),
                after=pc, sem="SD", dma=True)
            src_evs = [e1, prev_tail]
            ev_dtraw = e3
            ev_z = [e2]
        ev_conv, prev_tail_new, _doff = emit_conv_silu(c, "b", w, src_evs)
        ev_sp, evd = emit_dt_pipeline(c, "b", w, ev_dtraw)

        def attn_mk(cl):
            if is_win:
                return None
            lc = c0 + cl
            x1o = (lc - win) * CH

            def factory():
                eq2 = p.op("Q", lambda q, x1o=x1o: q.dma_start(
                    out=sb["attnf_in"][:], in_=T["attn_f"][x1o:x1o + CH, :]),
                    after=([ast["prev_ed"]] if ast["prev_ed"] else []) +
                    ast["prev_v"], sem="QD", dma=True)
                ev_add = eq2

                def emit_attn(mm_o, ev_r, B, rstd):
                    par = B // 4
                    ast_b = sb["attn_st" + ("2" if par else "")]
                    war = ([ast["prev_ed_par"][par]]
                           if ast["prev_ed_par"][par] else [])
                    e1 = p.rd("V", lambda v: v.scalar_tensor_tensor(
                        ast_b[:, 0:512], BK(PSUM, B + 0, 0, 512),
                        rstd, sb["attnf_in"][:, 0:512],
                        Alu.mult, Alu.add), [B + 0],
                        after=mm_o + [ev_r, ev_add] + war)
                    e2 = p.rd("V", lambda v: v.scalar_tensor_tensor(
                        ast_b[:, 512:1024], BK(PSUM, B + 1, 0, 512),
                        rstd, sb["attnf_in"][:, 512:1024],
                        Alu.mult, Alu.add), [B + 1],
                        after=mm_o + [ev_r, ev_add] + war)
                    ed = p.op("Q", lambda q: q.dma_start(
                        out=T["x1s"][x1o:x1o + CH, :], in_=ast_b[:]),
                        after=[e1, e2], sem="QD", dma=True)
                    ast["prev_v"] = [e1, e2]
                    ast["prev_ed"] = ed
                    ast["prev_ed_par"][par] = ed
                    return [ed]
                return emit_attn
            return factory

        tails = run_pairs(c, st, "b", list(reversed(range(w))), evd, ev_sp,
                          ev_conv, ev_z, lambda cl: not is_win, attn_mk,
                          xbase=_doff)
        prev_tail = prev_tail_new
        chunk_tail = []
        for bk in tails:
            chunk_tail.append(bk["ev_state"])
            chunk_tail += bk["attn"]
        prev_cons_t[ts] = chunk_tail + [prev_tail]
        prev_consumers = chunk_tail + [prev_tail]



def emit_B(c):
    nc, p, g, T, sb, PSUM, PSUMb = (c["nc"], c["p"], c["g"], c["T"], c["sb"],
                                    c["PSUM"], c["PSUMb"])
    asb, mark, reset = c["asb"], c["mark"], c["reset"]
    nxc = g["x1_chunks"]

    mark()
    w1q = asb("w1q", [128, 8 * DHID], BF16)
    ring = asb("ring", [128, 32 * 1024], BF16)
    x1t = asb("b_x1", [128, 4 * DM], F32)
    x1q = asb("b_x1q", [128, 8 * 512], BF16)
    u_s = asb("b_u", [128, DM], F32)
    xq_t = asb("b_xqt", [128, DM], BF16)
    h2w = asb("b_h2w", [128, 32 * 512], BF16)
    accV = asb("b_accV", [128, 512], BF16)
    accG = asb("b_accG", [128, 512], BF16)
    tre4 = [asb(f"b_tree{i}", [128, 512], BF16) for i in range(4)]
    tre = tre4[0]
    accV2 = asb("b_accV2", [128, 512], BF16)
    accG2 = asb("b_accG2", [128, 512], BF16)
    s2bc = asb("b_s2bc", [128, 512], BF16)
    s2tm = asb("b_s2tm", [128, 4], F32)
    stash = asb("b_stash", [128, 32], BF16)
    s2row = asb("b_s2row", [1, 512], F32)
    fcw = asb("b_fcw", [128, 32 * 3], F32)
    fcb = asb("b_fcb", [128, 32], F32)
    wsum = asb("b_wsum", [128, 1], F32)
    wtmp = asb("b_wtmp", [128, 1], F32)
    sw1 = asb("b_sw1", [128, 1], F32)
    sw1i = asb("b_sw1i", [128, 1], F32)
    bsm = asb("b_bsm", [128, 4 * 32], F32)
    hmask = asb("b_hm", [128, 4], F32)
    junk = asb("b_junk", [128, DM], BF16)
    wld = asb("b_wld", [128, DHID], F32,
              at=int(h2w.tensor.manual_sbuf_range[0]))

    def ld(dst, src, after=()):
        return p.op("Q", lambda q: q.dma_start(
            out=dst, in_=src, allow_slow_non_contiguous=True), after=after,
            sem="QD", dma=True)

    ld(fcw[:].rearrange("p (c k) -> p c k", c=32),
       T["fconv_w"].rearrange("c p k -> p c k"))
    ld(fcb[:].rearrange("p (c o) -> p c o", c=32),
       T["fconv_b"].rearrange("c p o -> p c o"))
    e_rm = p.op("V", lambda v: v.memset(ring[:], 0.0))

    ev_acc = None
    prev_ld = None
    for cc in range(8):
        e1 = ld(wld[:], T["w1T"].rearrange("(c p) j -> p c j", p=128)
                [:, cc, :], after=[prev_ld] if prev_ld else [])
        e2 = p.op("V", lambda v: v.reduce_sum(wtmp[:], wld[:], axis=AX.X,
                                              apply_absolute_value=True),
                  after=[e1] + ([ev_acc] if ev_acc else []))
        prev_ld = e2
        if cc == 0:
            ev_acc = p.op("V", lambda v: v.tensor_copy(wsum[:], wtmp[:]),
                          after=[e2])
        else:
            ev_acc = p.op("V", lambda v: v.tensor_add(wsum[:], wsum[:], wtmp[:]),
                          after=[e2])
    e3 = p.mm(lambda t: t.matmul(PSUM[0:1, 0:1], wsum[:], sb["ones_col"][:],
                                 start=True, stop=True), [0],
              after=[ev_acc], opens=True)
    e4 = p.rd("V", lambda v: v.tensor_scalar_mul(wtmp[0:1, 0:1],
                                                 PSUM[0:1, 0:1],
                                                 1.0 / (DHID * DM)),
              [0], after=[e3])
    e5 = p.op("V", lambda v: v.tensor_scalar_max(wtmp[0:1, 0:1], wtmp[0:1, 0:1],
                                                 1e-5), after=[e4])
    e6 = p.mm(lambda t: t.matmul(BK(PSUM, 1, 0, 1), sb["ones_row"][:],
                                 wtmp[0:1, 0:1], start=True, stop=True), [1],
              after=[e5], opens=True)
    e7 = p.rd("V", lambda v: v.tensor_copy(sw1i[:], BK(PSUM, 1, 0, 1)), [1],
              after=[e6])
    e8 = p.op("V", lambda v: v.reciprocal(sw1[:], sw1i[:]), after=[e7])
    ev_w1 = []
    for cc in range(8):
        e1 = ld(wld[:], T["w1T"].rearrange("(c p) j -> p c j", p=128)
                [:, cc, :], after=[prev_ld] + (ev_w1[-1:] if ev_w1 else []))
        e2 = p.op("V", lambda v: v.tensor_scalar(wld[:], wld[:], sw1[:], MAGIC,
                                                 Alu.mult, Alu.add),
                  after=[e1, e8])
        e2b = p.op("V", lambda v: v.tensor_scalar(wld[:], wld[:], MAGIC, 1.0,
                                                  Alu.subtract, Alu.min),
                   after=[e2])
        ev_w1.append(p.op("V", lambda v, cc=cc: v.tensor_scalar(
            w1q[:, cc * DHID:(cc + 1) * DHID], wld[:], -1.0, 0.0,
            Alu.max, Alu.add), after=[e2b]))
        prev_ld = ev_w1[-1]

    env = dict(w1q=w1q, ring=ring, x1t=x1t, x1q=x1q, u_s=u_s, xq_t=xq_t,
               h2w=h2w, accV=accV, accG=accG, tre=tre, s2bc=s2bc, s2tm=s2tm,
               fcw=fcw, fcb=fcb, sw1i=sw1i, bsm=bsm, hmask=hmask, junk=junk,
               stash=stash, s2row=s2row, tre4=tre4,
               accV2=accV2, accG2=accG2)
    e_sm = p.op("V", lambda v: v.memset(stash[:], 0.0))

    tiles = _tiles(0, nxc)
    ev_ring_t = []
    win_evs = {}

    def emit_front(ti, prev_front, prev_win, mm_war):
        (c0, c1) = tiles[ti]
        w = c1 - c0
        eq = ld(x1t[:, 0:w * DM].rearrange("p (c d) -> p c d", c=w),
                T["x1s"][c0 * CH:c1 * CH, :].rearrange("(c p) d -> p c d", p=128),
                after=prev_front)
        em = p.op("Q", lambda q, c0=c0, c1=c1, w=w: q.dma_start(
            out=hmask[:, 0:w].rearrange("p (c o) -> p c o", c=w),
            in_=T["halo_mask"][c0 * CH:c1 * CH, :]
            .rearrange("(c p) o -> p c o", p=128),
            allow_slow_non_contiguous=True), after=prev_front,
            sem="QD", dma=True)
        ev_q = []
        for cc in range(w):
            x1c = x1t[:, cc * DM:(cc + 1) * DM]
            smb = (cc % 4) * 32
            B_ = bsm

            def S(i, smb=smb):
                return B_[:, smb + i:smb + i + 1]
            b1e = p.op("V", lambda v, x1c=x1c, smb=smb: v.bn_stats(
                B_[:, smb + 16:smb + 22], x1c[:, 0:512]),
                after=[eq] + prev_win)
            b2e = p.op("V", lambda v, x1c=x1c, smb=smb: v.bn_stats(
                B_[:, smb + 22:smb + 28], x1c[:, 512:1024]),
                after=[eq] + prev_win)
            e1 = p.op("V", lambda v, smb=smb: v.bn_aggr(
                B_[:, smb + 4:smb + 6], B_[:, smb + 16:smb + 28]),
                after=[b1e, b2e])
            e5 = p.op("V", lambda v, smb=smb: v.tensor_scalar_add(
                S(6, smb), S(5, smb), EPS), after=[e1])
            e6 = p.op("S", lambda s, smb=smb: s.activation(
                S(7, smb), S(6, smb), Act.Ln), after=[e5])
            e7 = p.op("S", lambda s, smb=smb: s.activation(
                S(8, smb), S(7, smb), Act.Exp, scale=-0.5), after=[e6])
            e7b = p.op("V", lambda v, cc=cc, smb=smb: v.tensor_mul(
                S(8, smb), S(8, smb), hmask[:, cc:cc + 1]), after=[e7, em])
            e9 = p.op("V", lambda v, x1c=x1c, smb=smb: v.tensor_scalar(
                u_s[:], x1c, S(4, smb), S(8, smb), Alu.subtract, Alu.mult),
                after=[e7b])
            e10 = p.op("V", lambda v, smb=smb: v.reduce_max(
                S(9, smb), u_s[:], axis=AX.X, apply_absolute_value=True),
                after=[e9])
            e11 = p.op("V", lambda v, smb=smb: v.tensor_scalar_max(
                S(10, smb), S(9, smb), 1e-5), after=[e10])
            e12 = p.op("V", lambda v, smb=smb: v.reciprocal(
                S(11, smb), S(10, smb)), after=[e11])
            e13 = p.op("V", lambda v, smb=smb: v.tensor_scalar_mul(
                S(12, smb), S(11, smb), 127.0), after=[e12])
            e14 = p.op("V", lambda v, smb=smb: v.tensor_mul(
                S(13, smb), S(10, smb), sw1i[:]), after=[e11, e8])
            e15 = p.op("V", lambda v, smb=smb: v.tensor_scalar_mul(
                S(14, smb), S(13, smb), 1.0 / 127.0), after=[e14])
            e16 = p.op("V", lambda v, smb=smb: v.tensor_scalar(
                u_s[:], u_s[:], S(12, smb), MAGIC, Alu.mult, Alu.add),
                after=[e13, e9])
            e17 = p.op("V", lambda v, smb=smb: v.tensor_scalar(
                xq_t[:], u_s[:], MAGIC, S(14, smb), Alu.subtract, Alu.mult),
                after=[e16, e15])
            for b8 in range(8):
                bk = 6 + b8 % 2
                et = p.mm(lambda t, b8=b8, bk=bk: t.transpose(
                    PSUMb[:, bk * 1024:bk * 1024 + 128],
                    xq_t[:, b8 * 128:b8 * 128 + 128], sb["I128b"][:]),
                    [bk], after=[e17], opens=True)
                ev_q.append(p.rd("S", lambda s, b8=b8, bk=bk, cc=cc:
                                 s.activation(
                                     x1q[:, b8 * 512 + cc * 128:
                                         b8 * 512 + cc * 128 + 128],
                                     PSUMb[:, bk * 1024:bk * 1024 + 128],
                                     Act.Copy),
                                 [bk], after=[et] + mm_war))
        return ev_q

    fronts = {0: emit_front(0, [], [e_sm], [])}
    for ti, (c0, c1) in enumerate(tiles):
        w = c1 - c0
        ev_q = fronts[ti]
        rslot = (ti % 2) * 512
        ev_ring = []
        mm_last = None
        for j in range(32):
            bk = j % 4
            last = None
            for ccn in range(8):
                def mmf(t, j=j, ccn=ccn, bk=bk, w=w):
                    return t.matmul(
                        BK(PSUM, bk, 0, w * 128),
                        w1q[:, ccn * DHID + j * 128:ccn * DHID + j * 128 + 128],
                        x1q[:, ccn * 512:ccn * 512 + w * 128],
                        start=(ccn == 0), stop=(ccn == 7))
                last = p.mm(mmf, [bk], after=ev_q + [ev_w1[-1], e_rm],
                            opens=(ccn == 0))
            mm_last = last
            guard = win_evs.get(ti - 2, [e_sm, e_rm])
            ev_ring.append(p.rd("S", lambda s, j=j, bk=bk, w=w, rslot=rslot:
                                s.activation(
                                    ring[:, j * 1024 + rslot:j * 1024 + rslot + w * 128],
                                    BK(PSUM, bk, 0, w * 128), Act.Copy),
                                [bk], after=[last] + guard))
        ev_ring_t.append(ev_ring)
        if ti + 1 < len(tiles):
            fronts[ti + 1] = emit_front(ti + 1, ev_q,
                                        win_evs.get(ti - 1, [e_sm]),
                                        [mm_last])
        if ti > 0:
            win_evs[ti - 1] = emit_B1_window(c, env, tiles, ti - 1, ev_ring_t,
                                             win_evs.get(ti - 2, [e_sm]))
    win_evs[len(tiles) - 1] = emit_B1_window(c, env, tiles, len(tiles) - 1,
                                             ev_ring_t,
                                             win_evs.get(len(tiles) - 2, [e_sm]))
    p.barrier()
    reset()

    mark()
    w2q = asb("w2q", [128, 32 * DM], BF16)
    h2in = asb("b2_h2", [128, 2 * 32 * 512], BF16)
    x1r = asb("b2_x1", [128, 4 * DM], F32)
    outst = asb("b2_out", [128, 4 * DM], F32)
    s2t = asb("b2_s2", [128, 4], F32)
    w2sum = asb("b2_wsum", [128, 1], F32)
    w2tmp = asb("b2_wtmp", [128, 1], F32)
    sw2m = asb("b2_sw2m", [128, 1], F32)
    w2ld = asb("b2_wld", [128, DM], F32)

    ev_acc = None
    prev_ld = None
    for cc in range(32):
        e1 = ld(w2ld[:], T["w2T"].rearrange("(c p) j -> p c j", p=128)
                [:, cc, :], after=[prev_ld] if prev_ld else [])
        e2 = p.op("V", lambda v: v.reduce_sum(w2tmp[:], w2ld[:], axis=AX.X,
                                              apply_absolute_value=True),
                  after=[e1] + ([ev_acc] if ev_acc else []))
        prev_ld = e2
        if cc == 0:
            ev_acc = p.op("V", lambda v: v.tensor_copy(w2sum[:], w2tmp[:]),
                          after=[e2])
        else:
            ev_acc = p.op("V", lambda v: v.tensor_add(w2sum[:], w2sum[:],
                                                      w2tmp[:]), after=[e2])
    e3 = p.mm(lambda t: t.matmul(PSUM[0:1, 0:1], w2sum[:], sb["ones_col"][:],
                                 start=True, stop=True), [0], after=[ev_acc],
              opens=True)
    e4 = p.rd("V", lambda v: v.tensor_scalar_mul(w2tmp[0:1, 0:1],
                                                 PSUM[0:1, 0:1],
                                                 1.0 / (DHID * DM)),
              [0], after=[e3])
    e5 = p.op("V", lambda v: v.tensor_scalar_max(w2tmp[0:1, 0:1],
                                                 w2tmp[0:1, 0:1], 1e-5),
              after=[e4])
    e6 = p.mm(lambda t: t.matmul(BK(PSUM, 1, 0, 1), sb["ones_row"][:],
                                 w2tmp[0:1, 0:1], start=True, stop=True), [1],
              after=[e5], opens=True)
    e7 = p.rd("V", lambda v: v.tensor_copy(sw2m[:], BK(PSUM, 1, 0, 1)), [1],
              after=[e6])
    e8v = p.op("V", lambda v: v.reciprocal(w2tmp[:], sw2m[:]), after=[e7])
    ev_w2 = []
    for cc in range(32):
        e1 = ld(w2ld[:], T["w2T"].rearrange("(c p) j -> p c j", p=128)
                [:, cc, :], after=[prev_ld] + (ev_w2[-1:] if ev_w2 else []))
        e2 = p.op("V", lambda v: v.tensor_scalar(w2ld[:], w2ld[:], w2tmp[:],
                                                 MAGIC, Alu.mult, Alu.add),
                  after=[e1, e8v])
        e2b = p.op("V", lambda v: v.tensor_scalar(w2ld[:], w2ld[:], MAGIC, 1.0,
                                                  Alu.subtract, Alu.min),
                   after=[e2])
        ev_w2.append(p.op("V", lambda v, cc=cc: v.tensor_scalar(
            w2q[:, cc * DM:(cc + 1) * DM], w2ld[:], -1.0, 0.0, Alu.max,
            Alu.add), after=[e2b]))
        prev_ld = ev_w2[-1]

    kept = g["kept"]
    prev_out = []
    prev_loads = []
    for k in range(kept // 4):
        t0 = CH + 512 * k
        par = k % 2
        eh = ld(h2in[:, par * 32 * 512:(par + 1) * 32 * 512]
                .rearrange("p (c t) -> p c t", c=32),
                T["h2qs"][:, t0:t0 + 512].rearrange("(c p) t -> p c t", p=128),
                after=prev_loads[-2:] if len(prev_loads) >= 2 else [])
        ex = ld(x1r[:].rearrange("p (c d) -> p c d", c=4),
                T["x1s"][t0:t0 + 512, :]
                .rearrange("(c p) d -> p c d", p=128), after=prev_out)
        es = p.op("Q", lambda q, t0=t0: q.dma_start(
            out=s2t[:, 0:4].rearrange("p (c o) -> p c o", c=4),
            in_=T["s2fs"][t0:t0 + 512, :].rearrange("(c p) o -> p c o", p=128),
            allow_slow_non_contiguous=True), after=prev_out,
            sem="QD", dma=True)
        ea = p.op("V", lambda v: v.tensor_scalar_mul(s2t[:, 0:4], s2t[:, 0:4],
                                                     sw2m[:]), after=[es, e7])
        new_out = []
        for tc in range(4):
            for half in range(2):
                bk = (tc * 2 + half) % 8
                last = None
                for hc in range(32):
                    def mmf(t, tc=tc, half=half, hc=hc, bk=bk, par=par):
                        return t.matmul(
                            BK(PSUM, bk, 0, 512),
                            h2in[:, par * 32 * 512 + hc * 512 + tc * 128:
                                 par * 32 * 512 + hc * 512 + tc * 128 + 128],
                            w2q[:, hc * DM + half * 512:hc * DM + half * 512 + 512],
                            start=(hc == 0), stop=(hc == 31))
                    last = p.mm(mmf, [bk], after=[eh, ev_w2[-1]],
                                opens=(hc == 0))
                new_out.append(p.rd("V", lambda v, tc=tc, half=half, bk=bk:
                                    v.scalar_tensor_tensor(
                                        outst[:, tc * DM + half * 512:
                                              tc * DM + half * 512 + 512],
                                        BK(PSUM, bk, 0, 512), s2t[:, tc:tc + 1],
                                        x1r[:, tc * DM + half * 512:
                                            tc * DM + half * 512 + 512],
                                        Alu.mult, Alu.add), [bk],
                                    after=[last, ex, ea]))
        prev_out = [p.op("Q", lambda q, k=k: q.dma_start(
            out=T["y"][512 * k:512 * k + 512, :]
            .rearrange("(c p) d -> p c d", p=128),
            in_=outst[:].rearrange("p (c d) -> p c d", c=4)),
            after=new_out, sem="QD", dma=True)]
        prev_loads.append(eh)


def emit_B1_window(c, env, tiles, wi, ev_ring_t, prev_stash_rd):
    p, sb, PSUM, PSUMb, T = c["p"], c["sb"], c["PSUM"], c["PSUMb"], c["T"]
    ring, fcw, fcb = env["ring"], env["fcw"], env["fcb"]
    accV, accG, tre = env["accV"], env["accG"], env["tre"]
    tre4 = env["tre4"]
    h2w, s2bc, s2tm, bsm = env["h2w"], env["s2bc"], env["s2tm"], env["bsm"]
    stash, s2row = env["stash"], env["s2row"]
    sw1i = env["sw1i"]
    c0, c1 = tiles[wi]
    w = c1 - c0
    rslot = (wi % 2) * 512
    after0 = list(ev_ring_t[wi])
    if wi + 1 < len(tiles):
        after0 += ev_ring_t[wi + 1]
    nwtok = w * 128
    after0 = after0 + list(prev_stash_rd)
    ev_sil = []
    tree_evs = [None, None, None, None]
    last_e = {0: None, 1: None, 2: None, 3: None}
    ACCN = [accV, accG, env["accV2"], env["accG2"]]
    for blk in range(32):
        eng = "V"
        acc = ACCN[blk % 4]
        base = blk * 1024
        pe = last_e[blk % 4]
        e = p.op(eng, lambda v, blk=blk, base=base, acc=acc: v.tensor_scalar(
            acc[:, 0:nwtok], ring[:, base + rslot:base + rslot + nwtok],
            fcw[:, blk * 3 + 1:blk * 3 + 2], fcb[:, blk:blk + 1],
            Alu.mult, Alu.add), after=after0 + ([pe] if pe else []))
        e = p.op(eng, lambda v, blk=blk, acc=acc: v.scalar_tensor_tensor(
            acc[:, 0:1], stash[:, blk:blk + 1],
            fcw[:, blk * 3 + 0:blk * 3 + 1], acc[:, 0:1],
            Alu.mult, Alu.add), after=[e])
        e = p.op(eng, lambda v, blk=blk, base=base, acc=acc:
                 v.scalar_tensor_tensor(
            acc[:, 1:nwtok], ring[:, base + rslot:base + rslot + nwtok - 1],
            fcw[:, blk * 3 + 0:blk * 3 + 1], acc[:, 1:nwtok],
            Alu.mult, Alu.add), after=[e])
        pieces = []
        tlo = 0
        while tlo < nwtok:
            col = (rslot + 1 + tlo) % 1024
            run = min(nwtok - tlo, 1024 - col)
            pieces.append((tlo, col, run))
            tlo += run
        for (tlo, col, run) in pieces:
            e = p.op(eng, lambda v, blk=blk, base=base, col=col, run=run,
                     tlo=tlo, acc=acc: v.scalar_tensor_tensor(
                         acc[:, tlo:tlo + run],
                         ring[:, base + col:base + col + run],
                         fcw[:, blk * 3 + 2:blk * 3 + 3],
                         acc[:, tlo:tlo + run], Alu.mult, Alu.add),
                     after=[e])
        es = p.op("S", lambda s, blk=blk, nwtok=nwtok, acc=acc: s.activation(
            h2w[:, blk * 512:blk * 512 + nwtok], acc[:, 0:nwtok], Act.Silu),
            after=[e])
        last_e[blk % 4] = es
        ev_sil.append(es)
        lane = blk % 4
        trl = tre4[lane]
        if tree_evs[lane] is None:
            tree_evs[lane] = p.op("V", lambda v, nwtok=nwtok, blk=blk, trl=trl:
                                  v.scalar_tensor_tensor(
                trl[:, 0:nwtok], h2w[:, blk * 512:blk * 512 + nwtok], -1.0,
                h2w[:, blk * 512:blk * 512 + nwtok],
                Alu.mult, Alu.max), after=[es])
        else:
            t_ev = p.op("V", lambda v, blk=blk, nwtok=nwtok, trl=trl:
                        v.tensor_max(
                trl[:, 0:nwtok], trl[:, 0:nwtok],
                h2w[:, blk * 512:blk * 512 + nwtok]),
                after=[es, tree_evs[lane]])
            tree_evs[lane] = p.op("V", lambda v, blk=blk, nwtok=nwtok, trl=trl:
                                  v.scalar_tensor_tensor(
                trl[:, 0:nwtok], h2w[:, blk * 512:blk * 512 + nwtok], -1.0,
                trl[:, 0:nwtok], Alu.mult, Alu.max), after=[t_ev])
    ev_stash = p.op("V", lambda v: v.tensor_copy(
        stash[:].unsqueeze(2),
        ring.rearrange("p (b s) -> p b s", b=32)
        [:, :, rslot + nwtok - 1:rslot + nwtok].copy() if False else
        ring.rearrange("p (b s) -> p b s", b=32)[:, :, rslot + nwtok - 1:rslot + nwtok]),
        after=ev_sil + prev_stash_rd)
    e_c1 = p.op("V", lambda v, nwtok=nwtok: v.tensor_max(
        tre4[0][:, 0:nwtok], tre4[0][:, 0:nwtok], tre4[1][:, 0:nwtok]),
        after=[tree_evs[0], tree_evs[1]])
    e_c2 = p.op("V", lambda v, nwtok=nwtok: v.tensor_max(
        tre4[2][:, 0:nwtok], tre4[2][:, 0:nwtok], tre4[3][:, 0:nwtok]),
        after=[tree_evs[2], tree_evs[3]])
    tree_ev = p.op("V", lambda v, nwtok=nwtok: v.tensor_max(
        tre4[0][:, 0:nwtok], tre4[0][:, 0:nwtok], tre4[2][:, 0:nwtok]),
        after=[e_c1, e_c2])
    s2_evs = []
    prev = [tree_ev]
    for cc in range(w):
        et = p.mm(lambda t, cc=cc: t.transpose(
            PSUMb[:, 6 * 1024:6 * 1024 + 128],
            tre[:, cc * 128:cc * 128 + 128], sb["I128b"][:]), [6],
            after=prev, opens=True)
        e1 = p.rd("V", lambda v: v.reduce_max(bsm[:, 0:1],
                                              PSUMb[:, 6 * 1024:6 * 1024 + 128],
                                              axis=AX.X), [6], after=[et])
        e2 = p.op("V", lambda v: v.tensor_scalar_max(bsm[:, 1:2], bsm[:, 0:1],
                                                     1e-5), after=[e1])
        e4 = p.op("V", lambda v: v.reciprocal(bsm[:, 2:3], bsm[:, 1:2]),
                  after=[e2])
        e4b = p.op("V", lambda v: v.tensor_scalar_mul(bsm[:, 3:4], bsm[:, 2:3],
                                                      127.0), after=[e4])
        e5 = p.op("V", lambda v: v.tensor_scalar_mul(s2tm[:, 0:1], bsm[:, 1:2],
                                                     1.0 / 127.0), after=[e2])
        eqs = p.op("Q", lambda q, cc=cc, c0=c0: q.dma_start(
            out=T["s2fs"][(c0 + cc) * CH:(c0 + cc + 1) * CH, :],
            in_=s2tm[:, 0:1]), after=[e5], sem="QD", dma=True)
        et2 = p.mm(lambda t: t.transpose(PSUM[0:1, 6 * 512:6 * 512 + 128],
                                         bsm[:, 3:4], sb["I128f"][:]), [6],
                   after=[e4b, e1], opens=True)
        e6 = p.rd("V", lambda v, cc=cc: v.tensor_copy(
            s2row[0:1, cc * 128:cc * 128 + 128],
            PSUM[0:1, 6 * 512:6 * 512 + 128]), [6], after=[et2])
        s2_evs.append(e6)
        prev = [e6, eqs]
    e7 = p.mm(lambda t: t.matmul(BK(PSUM, 7, 0, nwtok), sb["ones_row"][:],
                                 s2row[0:1, 0:nwtok], start=True, stop=True),
              [7], after=s2_evs, opens=True)
    e8 = p.rd("V", lambda v: v.tensor_copy(s2bc[:, 0:nwtok],
                                           BK(PSUM, 7, 0, nwtok)), [7],
              after=[e7])
    q_evs = []
    for blk in range(32):
        e1 = p.op("V", lambda v, blk=blk: v.tensor_tensor(
            h2w[:, blk * 512:blk * 512 + nwtok],
            h2w[:, blk * 512:blk * 512 + nwtok], s2bc[:, 0:nwtok], Alu.mult),
            after=[e8, ev_sil[blk], ev_stash])
        q_evs.append(p.op("V", lambda v, blk=blk: v.tensor_scalar(
            h2w[:, blk * 512:blk * 512 + nwtok],
            h2w[:, blk * 512:blk * 512 + nwtok], MAGIC, MAGIC,
            Alu.add, Alu.subtract), after=[e1]))
    ed = p.op("Q", lambda q: q.dma_start(
        out=T["h2qs"][:, c0 * CH:c0 * CH + nwtok]
        .rearrange("(c p) t -> p c t", p=128),
        in_=h2w.rearrange("p (b s) -> p b s", b=32)[:, :, 0:nwtok]),
        after=q_evs, sem="QD", dma=True)
    return q_evs + [ed, ev_stash, e8] + prev




_NC_CACHE = {}


def _nc_for(gkey):
    if gkey not in _NC_CACHE:
        _NC_CACHE[gkey] = build_nc(geom(*gkey))
    return _NC_CACHE[gkey]


def _consts(g):
    import ml_dtypes
    bf = ml_dtypes.bfloat16
    r = np.arange(128)
    triu = (r[:, None] <= r[None, :]).astype(np.float32)
    tril = triu.T.copy()
    triu_s = (r[:, None] < r[None, :]).astype(np.float32)
    tril_s = triu_s.T.copy()
    TRIS = np.concatenate([triu, tril, tril_s, triu_s], axis=1)
    MASKS = np.concatenate([triu, tril], axis=1).astype(bf)
    SELB = np.zeros((NH, 8 * 128), np.float32)
    for ci in range(8):
        for pp in range(128):
            SELB[2 * ci + pp // 64, ci * 128 + pp] = 1.0
    SELB = SELB.astype(bf)
    return dict(
        I128b=np.eye(128).astype(bf), I128f=np.eye(128, dtype=np.float32),
        MASKS=np.ascontiguousarray(MASKS),
        SELB=SELB, ones_col=np.ones((128, 1), np.float32),
        ones_row=np.ones((1, 128), np.float32))


def _weight_inputs(inputs, g):
    f32 = np.float32
    w = {}
    for d, pre in (("f", "L0_attn_f_"), ("b", "L0_attn_b_")):
        inw = np.asarray(inputs[pre + "in_w"], f32)
        wT = np.zeros((DH, JP), f32)
        wT[:, :2320] = np.ascontiguousarray(inw.T)
        w[f"in_wT_{d}"] = wT
        w[f"conv_w_{d}"] = np.ascontiguousarray(
            np.asarray(inputs[pre + "conv_w"], f32).reshape(10, 128, 4))
        w[f"conv_b_{d}"] = np.ascontiguousarray(
            np.asarray(inputs[pre + "conv_b"], f32).reshape(10, 128, 1))
        w[f"dt_bias_{d}"] = np.asarray(inputs[pre + "dt_bias"], f32).reshape(NH, 1)
        w[f"A_log_{d}"] = np.asarray(inputs[pre + "A_log"], f32).reshape(NH, 1)
        D = np.asarray(inputs[pre + "D"], f32)
        idx = 2 * np.arange(8)[None, :] + (np.arange(128)[:, None] // 64)
        w[f"D64_{d}"] = np.ascontiguousarray(D[idx]).astype(f32)
        w[f"outp_{d}"] = np.asarray(inputs[pre + "outp_w"], f32)
        w[f"gnorm_row_{d}"] = np.asarray(inputs[pre + "gnorm_w"], f32).reshape(1, DIN)
    w["norm_col"] = np.asarray(inputs["L0_attn_norm_w"], f32).reshape(DM, 1)
    w["op_wT"] = np.ascontiguousarray(np.asarray(inputs["L0_attn_op_w"], f32).T)
    w["opb_row"] = np.asarray(inputs["L0_attn_op_b"], f32).reshape(1, DM)
    w["w1T"] = np.ascontiguousarray(np.asarray(inputs["L0_ffn_w1"], f32).T)
    w["w2T"] = np.ascontiguousarray(np.asarray(inputs["L0_ffn_w2"], f32).T)
    w["fconv_w"] = np.ascontiguousarray(
        np.asarray(inputs["L0_ffn_conv_w"], f32).reshape(DHID, 3).reshape(32, 128, 3))
    w["fconv_b"] = np.ascontiguousarray(
        np.asarray(inputs["L0_ffn_conv_b"], f32).reshape(32, 128, 1))
    return w


def _core_inputs(xfull, b, s, e, L, g, wshared, consts):
    f32 = np.float32
    span_rows = g["span"] * CH
    x1_rows = g["x1_chunks"] * CH
    lo = s - (g["win"] + 1) * CH
    xc = np.zeros((span_rows, DM), f32)
    a = max(lo, 0)
    bnd = min(lo + span_rows, L)
    xc[a - lo:bnd - lo] = xfull[b, a:bnd]
    hm = np.ones((x1_rows, 1), f32)
    if s == 0:
        hm[127, 0] = 0.0
    if e == L:
        hm[(g["kept"] + 1) * CH, 0] = 0.0
    d = dict(xc=xc, halo_mask=hm)
    d.update(wshared)
    d.update(consts)
    return d


def kernel(**inputs):
    g = geom(32, 4)
    L = 16384
    x = np.asarray(inputs["L0_attn_x"], np.float32)
    nc = _nc_for((32, 4))
    wshared = _weight_inputs(inputs, g)
    consts = _consts(g)
    in_maps = []
    metas = []
    for core in range(8):
        b = core // 4
        s = (core % 4) * 4096
        e = s + 4096
        in_maps.append(_core_inputs(x, b, s, e, L, g, wshared, consts))
        metas.append((b, s, e))
    res = run_bass_kernel_spmd(nc, in_maps, core_ids=list(range(8)))
    out = np.empty((2, L, DM), np.float32)
    for core, (b, s, e) in enumerate(metas):
        out[b, s:e] = res.results[core]["y"]
    return out

